# version 23
# baseline (speedup 1.0000x reference)
import os
import numpy as np
import concourse.bass as bass
import concourse.mybir as mybir
from concourse.bass_utils import run_bass_kernel_spmd
from contextlib import ExitStack

F32 = mybir.dt.float32
F32R = mybir.dt.float32r
BF16 = mybir.dt.bfloat16
I32 = mybir.dt.int32
U32 = mybir.dt.uint32
AF = mybir.ActivationFunctionType
ALU = mybir.AluOpType
AX = mybir.AxisListType

C = 1024
CCN = 8
S = 16
H = 16
EPS_LN = 1e-5
EPS_DEN = 1e-6


def build(TC, taps=(), upto=None):
    TCH = TC + 2
    NTB = TC // 512
    NTS = TC // 128
    NCK = TC // 256
    nc = bass.Bass("TRN2", target_bir_lowering=False)
    es = ExitStack()

    def di(name, shape, dt=F32):
        return nc.dram_tensor(name, shape, dt, kind="ExternalInput").ap()

    xfh_in = di("xfh", [CCN, 128, TCH])
    W = {n: di(n, [CCN, CCN, 128, 128], F32R) for n in
         ["w1t", "v1t", "w2t", "wqt", "wkt", "wvt", "wrt", "wkkt", "wvvt"]}
    wd_in = di("wd", [C, C])
    V = {n: di(n, [C]) for n in
         ["b_w1", "b_v1", "b_w2", "b_f", "b_q", "b_k", "b_v",
          "gamma", "beta", "mu_r", "mu_k", "dssm"]}
    grow_in = di("gamma_row", [1, C])
    brow_in = di("beta_row", [1, C])
    cw_in = di("cw", [C, 3])
    A_in = di("A_", [C, S])
    B_in = di("B_", [C, S])
    C_in = di("C_", [C, S])
    eye_in = di("eye", [128, 128])
    selw_in = di("selw", [128, 8])
    selm_in = di("selm", [128, 8])
    kvsel_in = di("kvsel", [128, 8])
    pvsel_in = di("pvsel", [128, 8])
    hmask_in = di("hmask", [128, 2])

    out = nc.dram_tensor("out", [TC, C], F32, kind="ExternalOutput").ap()
    tap_shapes = {"xn0": [CCN, 128, TC], "x1": [CCN, 128, TCH],
                  "xconv": [CCN, 128, TC], "sloc": [128, 128],
                  "hin": [128, 128], "yssm": [CCN, 128, TC],
                  "prod": [CCN, 128, TC], "x1r": [CCN, 128, TC],
                  "xn1": [CCN, 128, TC], "kvg": [128, 528],
                  "qf": [CCN, 128, TC], "att": [TC, C],
                  "xd": [CCN, 128, TC], "rwkv": [CCN, 128, TC],
                  "wdt": [CCN, CCN, 128, 128]}
    tap_aps = {t: nc.dram_tensor("tap_" + t, tap_shapes[t], F32,
                                 kind="ExternalOutput").ap() for t in taps}

    TCP = TCH + 6
    x1_h = nc.dram_tensor("x1_h", [CCN, 128, TCP], F32).ap()
    x1f_h = nc.dram_tensor("x1f_h", [CCN, 128, TC], F32).ap()
    at0_h = nc.dram_tensor("at0_h", [CCN, 128, TC], F32).ap()
    kf_h = nc.dram_tensor("kf_h", [CCN, 128, TC], F32).ap()
    vf_h = nc.dram_tensor("vf_h", [CCN, 128, TC], F32).ap()
    attn_h = nc.dram_tensor("attn_h", [TC, C], F32).ap()
    r_h = nc.dram_tensor("r_h", [CCN, 128, TC], F32).ap()
    wdt_h = nc.dram_tensor("wdt_h", [CCN, CCN, 128, 128], F32R).ap()
    cc1_in = nc.dram_tensor("cc1_in", [128, 128], F32).ap()
    cc1_out = nc.dram_tensor("cc1_out", [8 * 128, 128], F32).ap()
    cc2_in = nc.dram_tensor("cc2_in", [128, 528], F32).ap()
    cc2_out = nc.dram_tensor("cc2_out", [8 * 128, 528], F32).ap()

    with es:
        sbt = lambda n, shp, dt=F32: es.enter_context(nc.sbuf_tensor(n, shp, dt))
        SL1 = sbt("SL1", [128, CCN * TC])
        SL2 = sbt("SL2", [128, CCN * TC])
        WA = sbt("WA", [128, CCN * 128], F32R)
        WB = sbt("WB", [128, CCN * 128], F32R)
        TA = sbt("TA", [128, 2048])
        TB_ = sbt("TB_", [128, 2048])
        TCm = sbt("TCm", [128, max(TCH, 2048)])
        TI = sbt("TI", [128, 2048], I32)
        TU = sbt("TU", [128, 2048], U32)
        AR1 = sbt("AR1", [128, CCN * 256])
        AR2 = sbt("AR2", [128, CCN * 256])
        stat1 = sbt("stat1", [128, 512])
        stat2 = sbt("stat2", [128, 512])
        eye = sbt("eye_s", [128, 128])
        ones_ln = sbt("ones_ln", [128, 128])
        ones1 = sbt("ones1", [1, 128])
        onescol = sbt("onescol", [128, 1])
        epsc = sbt("epsc", [128, 1])
        zcol = sbt("zcol", [128, 1])
        vt = {n: sbt("t_" + n, [128, CCN]) for n in V}
        gbc = TI[:, 0:1024].bitcast(F32)
        bbc = TI[:, 1024:2048].bitcast(F32)
        cwt = sbt("cwt", [128, 3 * CCN])
        At = sbt("At", [128, S * CCN])
        abar = sbt("abar", [128, S * CCN])
        lnat = sbt("lnat", [128, S * CCN])
        cbt = sbt("cbt", [128, S * CCN])
        sloc = sbt("sloc", [128, 128])
        hin = sbt("hin", [128, 128])
        eye_b = sbt("eye_b", [128, 128], BF16)
        abK = sbt("abK", [128, S * CCN])
        dg = [sbt("dg0", [128, 128], BF16), sbt("dg1", [128, 128], BF16)]
        selwt = sbt("selwt", [128, 8])
        selmt = sbt("selmt", [128, 8])
        kvselt = sbt("kvselt", [128, 8])
        pvselt = sbt("pvselt", [128, 8])
        hmaskt = sbt("hmaskt", [128, 2])
        kvg = sbt("kvg", [128, 512])
        ksg = sbt("ksg", [128, 8])
        kslo = sbt("kslo", [128, 8])
        ksbd = sbt("ksbd", [128, CCN * 16])
        xprev = sbt("xprev", [128, CCN])
        xnh = sbt("xnh", [128, 2 * CCN])
        rden = sbt("rden", [128, 16 * NTS])
        ps = [es.enter_context(nc.psum_tensor("ps%d" % i, [128, 512], F32))
              for i in range(8)]
        kvbd = AR1[:, 0:1024]
        kvp = AR2[:, 0:512]
        pack = AR2[:, 512:1040]
        iota_f = AR1[:, 0:TC]
        grow = TA[0:1, 0:1024]
        brow = TA[0:1, 1024:2048]
        Bt = TB_[:, 0:128]
        Ct = TB_[:, 128:256]

        ops = []
        cnt = {}

        def bump(s, n=1):
            cnt[s] = cnt.get(s, 0) + n
            return cnt[s]

        def cur(s):
            return cnt.get(s, 0)

        def op(eng, emit, waits=(), incs=()):
            ops.append((eng, tuple(waits), emit, tuple(incs)))

        ENGS = ["pe", "dve", "act", "pool", "sp"]
        stage_marks = []

        def barrier():
            stage_marks.append(len(ops))
            dw = [(s, cur(s)) for s in cnt if s.startswith(("ld", "st", "wl"))
                  or s in ("out", "cc", "cc2")]
            op("sp", None, waits=dw)
            tgts = []
            for e in ENGS:
                b = "bar_" + e
                op(e, lambda g: g.drain(), incs=[(b, 1)])
                tgts.append((b, bump(b)))
            for e in ENGS:
                op(e, None, waits=tgts)

        def dma(src, dst, sem="ld", eng="sp", waits=()):
            op(eng, lambda g, s=src, d=dst: g.dma_start(out=d, in_=s),
               waits=waits, incs=[(sem, 16)])
            return bump(sem, 16)

        def allw():
            return [("pe", cur("pe")), ("dve", cur("dve")), ("act", cur("act")),
                    ("pool", cur("pool"))]

        def wslice(wname, dc, dst, waits=()):
            src = (W[wname] if wname in W else wdt_h)[dc].rearrange(
                "c p m -> p c m")
            sem = "wlA" if dst is WA else "wlB"
            dma(src, dst[:, :].rearrange("p (c m) -> p c m", m=128), sem=sem,
                waits=waits)
            return (sem, cur(sem))

        def sl1(cc):
            return SL1[:, cc * TC:(cc + 1) * TC]

        def sl2(cc):
            return SL2[:, cc * TC:(cc + 1) * TC]

        def proj_pipe(wname, mov_fn, post_fn, nblk, bw, movwaits=(),
                      banks=(2, 3, 4, 5), extra_mm=None, extra_waits=None):
            nb = len(banks)
            pe_after_dc = {}
            freed = []
            for dc in range(CCN):
                wbuf = WA if dc % 2 == 0 else WB
                wwaits = [("pe", pe_after_dc[dc - 2] if dc >= 2 else cur("pe"))]
                wl = wslice(wname, dc, wbuf, waits=wwaits)
                for blk in range(nblk):
                    g = dc * nblk + blk
                    waits = [wl]
                    if g == 0:
                        waits += list(movwaits)
                    if g >= nb:
                        waits.append(freed[g - nb])
                    bank = banks[g % nb]
                    if extra_waits is not None:
                        waits += extra_waits(dc, blk)

                    def e_mm(pe, w=wbuf, b=bank, blkx=blk, dcx=dc, w_=bw):
                        mm = None
                        for cc in range(CCN):
                            mm = pe.matmul(ps[b][:, :w_],
                                           w[:, cc * 128:(cc + 1) * 128],
                                           mov_fn(cc, blkx).bitcast(F32R),
                                           start=(cc == 0), stop=(cc == CCN - 1))
                        if extra_mm is not None:
                            mm = extra_mm(pe, dcx, blkx, b) or mm
                        return mm
                    op("pe", e_mm, waits=waits, incs=[("pe", 1)])
                    bump("pe")
                    freed.append(post_fn(dc, blk, bank, g))
                pe_after_dc[dc] = cur("pe")
            return cur("pe")

        def emit_quant(src, dst, s1, s2, s3, waits):
            def emit(v):
                G = C // 32
                v.drain()
                v.tensor_reduce(stat1[:, :G], src.rearrange("p (g k) -> p g k", k=32),
                                axis=AX.X, op=ALU.max, apply_absolute_value=True)
                v.drain()
                v.tensor_scalar(stat1[:, G:2 * G], stat1[:, :G], 1.0 / 7.5, 1e-30,
                                ALU.mult, ALU.max)
                v.drain()
                v.reciprocal(stat2[:, :G], stat1[:, G:2 * G])
                v.drain()
                rbc = stat2[:, :G].unsqueeze(2).broadcast_to([128, G, 32])
                v.tensor_tensor(s1.rearrange("p (g k) -> p g k", k=32),
                                src.rearrange("p (g k) -> p g k", k=32), rbc,
                                ALU.mult)
                v.drain()
                v.tensor_scalar(TU[:, :1024], s1.bitcast(U32), 23, 255,
                                ALU.logical_shift_right, ALU.bitwise_and)
                v.drain()
                v.tensor_scalar(s2, TU[:, :1024], 127.0, -1.0, ALU.max, ALU.mult)
                v.drain()
                v.tensor_scalar(TU[:, :1024], s2, 257.0, None, ALU.add)
                v.drain()
                v.tensor_scalar(TU[:, :1024], TU[:, :1024], 23, None,
                                ALU.logical_shift_left)
                v.drain()
                v.tensor_tensor(s1, s1, TU[:, :1024].bitcast(F32), ALU.mult)
                v.drain()
                v.tensor_copy(TI[:, :1024], s1)
                v.drain()
                v.tensor_copy(s3, TI[:, :1024])
                v.drain()
                v.tensor_tensor(dst, s1, s3, ALU.subtract)
                v.drain()
                v.tensor_scalar(s1, dst, 0.5, None, ALU.is_ge)
                v.drain()
                v.tensor_tensor(s3, s3, s1, ALU.add)
                v.drain()
                v.tensor_scalar(s1, dst, -0.5, None, ALU.is_le)
                v.drain()
                v.tensor_tensor(s1, s3, s1, ALU.subtract)
                v.drain()
                v.tensor_scalar(TU[:, :1024], s2, -1.0, -3.0, ALU.mult, ALU.add)
                v.drain()
                v.tensor_scalar(TU[:, :1024], TU[:, :1024], 23, None,
                                ALU.logical_shift_left)
                v.drain()
                v.tensor_tensor(s1, s1, TU[:, :1024].bitcast(F32), ALU.mult)
                v.drain()
                sbc = stat1[:, G:2 * G].unsqueeze(2).broadcast_to([128, G, 32])
                return v.tensor_tensor(dst.rearrange("p (g k) -> p g k", k=32),
                                       s1.rearrange("p (g k) -> p g k", k=32),
                                       sbc, ALU.mult)
            op("dve", emit, waits=waits, incs=[("dve", 1)])
            return bump("dve")

        dma(eye_in, eye[:])
        dma(selw_in, selwt[:])
        dma(selm_in, selmt[:])
        dma(kvsel_in, kvselt[:])
        dma(pvsel_in, pvselt[:])
        dma(hmask_in, hmaskt[:])
        dma(grow_in, grow)
        dma(brow_in, brow)
        for n in V:
            for cc in range(CCN):
                dma(V[n][cc * 128:(cc + 1) * 128].rearrange("(p o) -> p o", o=1),
                    vt[n][:, cc:cc + 1])
        for cc in range(CCN):
            dma(cw_in[cc * 128:(cc + 1) * 128, :], cwt[:, cc * 3:(cc + 1) * 3])
            dma(A_in[cc * 128:(cc + 1) * 128, :], At[:, cc * S:(cc + 1) * S])
            dma(B_in[cc * 128:(cc + 1) * 128, :], Bt[:, cc * S:(cc + 1) * S])
            dma(C_in[cc * 128:(cc + 1) * 128, :], Ct[:, cc * S:(cc + 1) * S])
        ld0 = cur("ld")

        def e_ms(g):
            g.memset(ones_ln[:], 1.0 / 1024.0)
            g.memset(ones1[:], 1.0)
            g.memset(onescol[:], 1.0)
            g.memset(epsc[:], EPS_LN)
            g.memset(hin[:], 0.0)
            g.memset(ksbd[:], 0.0)
            g.memset(zcol[:], 0.0)
            g.drain()
            return g.memset(xprev[:], 0.0)
        op("pool", e_ms, incs=[("pool", 1)])
        bump("pool")

        def e_ab(g):
            g.activation(abar[:], At[:], AF.Sigmoid)
            g.drain()
            return g.activation(lnat[:], abar[:], AF.Ln)
        op("act", e_ab, waits=[("ld", ld0)], incs=[("act", 1)])
        bump("act")
        op("dve", lambda g: g.tensor_tensor(cbt[:], Bt, Ct, ALU.mult),
           waits=[("ld", ld0)], incs=[("dve", 1)])
        bump("dve")
        op("dve", lambda g: (g.drain(), g.tensor_copy(eye_b[:], eye[:]))[1],
           waits=[("ld", ld0)], incs=[("dve", 1)])
        bump("dve")
        cbt_dve = cur("dve")
        barrier()

        chunks = [(i * 256, 256) for i in range(NCK)] + \
                 [(NCK * 256, TCH - NCK * 256)]
        a1a_nrm = {}
        for ci, (c0, cwd) in enumerate(chunks):
            lw = allw() if ci == 0 else [("dve", a1a_nrm[ci - 1])]
            for cc in range(CCN):
                l0 = dma(xfh_in[cc, :, c0:c0 + cwd],
                         AR1[:, cc * 256:cc * 256 + cwd], sem="ldc", waits=lw)
                lw = ()

            def e_sq(a, w_=cwd):
                r = None
                for cc in range(CCN):
                    a.drain()
                    r = a.activation(AR2[:, cc * 256:cc * 256 + w_],
                                     AR1[:, cc * 256:cc * 256 + w_], AF.Square)
                return r
            op("act", e_sq, waits=[("ldc", cur("ldc")), ("dve", cur("dve"))],
               incs=[("act", 1)])
            a_sq = bump("act")

            def e_st(g, w_=cwd):
                for cc in range(CCN):
                    g.matmul(ps[4][:, :w_], ones_ln[:],
                             AR1[:, cc * 256:cc * 256 + w_],
                             start=(cc == 0), stop=(cc == CCN - 1))
                mm = None
                for cc in range(CCN):
                    mm = g.matmul(ps[5][:, :w_], ones_ln[:],
                                  AR2[:, cc * 256:cc * 256 + w_],
                                  start=(cc == 0), stop=(cc == CCN - 1))
                return mm
            op("pe", e_st, waits=[("act", a_sq)], incs=[("pe", 1)])
            p_st = bump("pe")

            def e_m2(a, w_=cwd):
                a.drain()
                return a.activation(stat1[:, :w_], ps[4][:, :w_], AF.Square)
            op("act", e_m2, waits=[("pe", p_st)], incs=[("act", 1)])
            a_m2 = bump("act")

            def e_var(v, w_=cwd):
                v.drain()
                return v.tensor_tensor(stat2[:, :w_], ps[5][:, :w_],
                                       stat1[:, :w_], ALU.subtract)
            op("dve", e_var, waits=[("act", a_m2)], incs=[("dve", 1)])
            d_v = bump("dve")
            op("act", lambda a, w_=cwd: (a.drain(), a.activation(
                stat1[:, :w_], stat2[:, :w_], AF.Sqrt, bias=epsc[:],
                scale=1.0))[1], waits=[("dve", d_v)], incs=[("act", 1)])
            a_sd = bump("act")
            op("dve", lambda v, w_=cwd: (v.drain(), v.reciprocal(
                stat2[:, :w_], stat1[:, :w_]))[1],
               waits=[("act", a_sd)], incs=[("dve", 1)])
            d_r = bump("dve")
            for cc in range(CCN):
                def e_n(v, ccx=cc, w_=cwd, c0x=c0):
                    v.drain()
                    v.tensor_tensor(AR2[:, ccx * 256:ccx * 256 + w_],
                                    AR1[:, ccx * 256:ccx * 256 + w_],
                                    ps[4][:, :w_], ALU.subtract)
                    v.drain()
                    v.tensor_tensor(AR2[:, ccx * 256:ccx * 256 + w_],
                                    AR2[:, ccx * 256:ccx * 256 + w_],
                                    stat2[:, :w_], ALU.mult)
                    v.drain()
                    lo, hi = c0x, c0x + w_
                    rlo, rhi = max(lo, 1), min(hi, TC + 1)
                    r = None
                    if rlo < rhi:
                        r = v.tensor_copy(sl1(ccx)[:, rlo - 1:rhi - 1]
                                          .bitcast(F32R),
                                          AR2[:, ccx * 256 + (rlo - lo):
                                              ccx * 256 + (rhi - lo)])
                    if lo == 0:
                        v.drain()
                        r = v.tensor_copy(xnh[:, 2 * ccx:2 * ccx + 1]
                                          .bitcast(F32R),
                                          AR2[:, ccx * 256:ccx * 256 + 1])
                    if hi == TCH:
                        v.drain()
                        r = v.tensor_copy(xnh[:, 2 * ccx + 1:2 * ccx + 2]
                                          .bitcast(F32R),
                                          AR2[:, ccx * 256 + (TCH - 1 - lo):
                                              ccx * 256 + (TCH - lo)])
                    return r
                op("dve", e_n, waits=[("dve", d_r)], incs=[("dve", 1)])
                bump("dve")
            a1a_nrm[ci] = cur("dve")
        a1a_dve = cur("dve")
        if "xn0" in taps:
            for cc in range(CCN):
                dma(sl1(cc), tap_aps["xn0"][cc], sem="st",
                    waits=[("dve", cur("dve"))])
        barrier()

        x1_st = {}
        dvh_cnt = {}

        def extra_x1(pe, dc, blk, bank):
            if blk != 0:
                return None
            mm = None
            wbuf = WA if dc % 2 == 0 else WB
            for cc in range(CCN):
                mm = pe.matmul(ps[6][:, 0:2], wbuf[:, cc * 128:(cc + 1) * 128],
                               xnh[:, 2 * cc:2 * cc + 2].bitcast(F32R),
                               start=(cc == 0), stop=(cc == CCN - 1))
            return mm

        def post_x1(dc, blk, bank, g):
            reg = g % 4
            pw = [("pe", cur("pe"))]
            if g >= 4:
                pw.append(("stx1r%d" % reg, cur("stx1r%d" % reg)))

            def e_ev(a, dcx=dc, b=bank, r=reg):
                a.drain()
                return a.activation(TA[:, r * 512:r * 512 + 512],
                                    ps[b][:, :512], AF.Identity,
                                    bias=vt["b_w1"][:, dcx:dcx + 1], scale=1.0)
            op("act", e_ev, waits=pw, incs=[("act", 1)])
            a_e = bump("act")
            dma(TA[:, reg * 512:reg * 512 + 512],
                x1_h[dc, :, 1 + blk * 512:1 + (blk + 1) * 512],
                sem="stx1r%d" % reg, waits=[("act", a_e)])
            if blk == 0:
                def e_evh(v, dcx=dc):
                    v.drain()
                    v.tensor_scalar(stat1[:, 440:442], ps[6][:, 0:2],
                                    vt["b_w1"][:, dcx:dcx + 1], None, ALU.add)
                    v.drain()
                    v.tensor_scalar(stat1[:, 440:441], stat1[:, 440:441],
                                    hmaskt[:, 0:1], None, ALU.mult)
                    v.drain()
                    return v.tensor_scalar(stat1[:, 441:442], stat1[:, 441:442],
                                           hmaskt[:, 1:2], None, ALU.mult)
                op("dve", e_evh, waits=[("pe", cur("pe")),
                                        ("stx1h", cur("stx1h"))],
                   incs=[("dve", 1)])
                d_h = bump("dve")
                dvh_cnt[dc] = d_h
                op("sp", lambda g, dcx=dc: g.dma_start(
                    out=x1_h[dcx, :, 0:1], in_=stat1[:, 440:441],
                    allow_slow_non_contiguous=True),
                   waits=[("dve", d_h)], incs=[("stx1h", 16)])
                bump("stx1h", 16)
                op("sp", lambda g, dcx=dc: g.dma_start(
                    out=x1_h[dcx, :, TC + 1:TC + 2], in_=stat1[:, 441:442],
                    allow_slow_non_contiguous=True),
                   incs=[("stx1h", 16)])
                bump("stx1h", 16)
            if blk == NTB - 1:
                x1_st[dc] = (tuple((("stx1r%d" % r), cur("stx1r%d" % r))
                             for r in range(4)), cur("stx1h"))
            return ("act", a_e)

        proj_pipe("w1t", lambda cc, blk: sl1(cc)[:, blk * 512:(blk + 1) * 512],
                  post_x1, NTB, 512, movwaits=[("dve", a1a_dve)],
                  extra_mm=extra_x1,
                  extra_waits=lambda dc, blk: (
                      [("dve", dvh_cnt[dc - 1])] if blk == 0 and dc >= 1
                      else []))
        a1b_act = cur("act")
        qprev = cbt_dve
        for cc in range(CCN):
            w0 = dma(wd_in[cc * 128:(cc + 1) * 128, :],
                     TI[:, 1024:2048].bitcast(F32),
                     sem="ldw0", waits=[("dve", qprev),
                                        ("st", cur("st"))])
            q = emit_quant(TI[:, 1024:2048].bitcast(F32),
                           TB_[:, :1024], TCm[:, :1024],
                           TU[:, 1024:2048].bitcast(F32),
                           TB_[:, 1024:2048],
                           waits=[("ldw0", w0)])
            qprev = q
            for dcb in range(CCN):
                bnk, off = dcb % 2, (dcb // 2 % 2) * 128

                def e_tr(g, d=dcb, b=bnk, o=off):
                    return g.transpose(ps[b][:, o:o + 128],
                                       TB_[:, d * 128:(d + 1) * 128], eye[:])
                op("pe", e_tr, waits=[("dve", q), ("act", cur("act"))],
                   incs=[("pe", 1)])
                tp = bump("pe")

                def e_ev(a, d=dcb, b=bnk, o=off):
                    return a.activation(TCm[:, d * 128:(d + 1) * 128],
                                        ps[b][:, o:o + 128],
                                        AF.Copy, bias=0.0, scale=1.0)
                op("act", e_ev, waits=[("pe", tp), ("st", cur("st"))],
                   incs=[("act", 1)])
                ev = bump("act")
                dma(TCm[:, dcb * 128:(dcb + 1) * 128],
                    wdt_h[cc, dcb].bitcast(F32), sem="st", waits=[("act", ev)])
        if "wdt" in taps:
            dma(wdt_h[:, :, :, :], tap_aps["wdt"], sem="st",
                waits=[("st", cur("st"))])
        if "x1" in taps:
            for cc in range(CCN):
                dma(x1_h[cc, :, 0:TCH], tap_aps["x1"][cc], sem="st",
                    waits=[("stx1r%d" % r, cur("stx1r%d" % r))
                           for r in range(4)] +
                          [("stx1h", cur("stx1h"))])

        BQ = [TI[:, 0:TC].bitcast(F32), TU[:, 0:TC].bitcast(F32)]
        HB = [AR1[:, 0:TC // 2].bitcast(BF16), AR2[:, 0:TC // 2].bitcast(BF16)]
        DB = [TI[:, 0:TC // 2].bitcast(BF16), TU[:, 0:TC // 2].bitcast(BF16)]
        dvescan = {}
        ymm = {}
        yev = {}
        pend_x = []
        c1p = {}
        c23d = {}
        silua = {}
        for cc in range(CCN):
            lw = list(x1_st[cc][0]) + [("stx1h", x1_st[cc][1])]
            if cc == 0:
                lw += [("dve", cur("dve")), ("act", cur("act")),
                       ("pool", cur("pool")), ("pe", cur("pe")),
                       ("st", cur("st"))]
            else:
                lw += [("pool", c1p[cc - 1]), ("dve", c23d[cc - 1])]
            l0 = dma(x1_h[cc, :, 0:TC], TB_[:, 0:TC], waits=lw)
            l0 = dma(x1_h[cc, :, TC:TCH], stat2[:, 448:450])

            def e_c1(g, ccx=cc):
                g.drain()
                return g.tensor_scalar(TA[:, :TC], TB_[:, 0:TC],
                                       cwt[:, ccx * 3:ccx * 3 + 1], 0.0,
                                       ALU.mult, ALU.add)
            c1w = [("ld", l0)]
            if cc > 0:
                c1w.append(("act", silua[cc - 1]))
            else:
                c1w += [("stx1r%d" % r, cur("stx1r%d" % r)) for r in range(4)]
            op("pool", e_c1, waits=c1w, incs=[("pool", 1)])
            pl = bump("pool")

            def e_c23(v, ccx=cc):
                v.drain()
                v.scalar_tensor_tensor(TCm[:, 0:TC - 1], TB_[:, 1:TC],
                                       cwt[:, ccx * 3 + 1:ccx * 3 + 2],
                                       TA[:, 0:TC - 1], ALU.mult, ALU.add)
                v.drain()
                v.scalar_tensor_tensor(TCm[:, TC - 1:TC],
                                       stat2[:, 448:449],
                                       cwt[:, ccx * 3 + 1:ccx * 3 + 2],
                                       TA[:, TC - 1:TC], ALU.mult, ALU.add)
                v.drain()
                v.scalar_tensor_tensor(TA[:, 0:TC - 2], TB_[:, 2:TC],
                                       cwt[:, ccx * 3 + 2:ccx * 3 + 3],
                                       TCm[:, 0:TC - 2], ALU.mult, ALU.add)
                v.drain()
                return v.scalar_tensor_tensor(TA[:, TC - 2:TC],
                                              stat2[:, 448:450],
                                              cwt[:, ccx * 3 + 2:ccx * 3 + 3],
                                              TCm[:, TC - 2:TC],
                                              ALU.mult, ALU.add)
            c1p[cc] = pl
            op("dve", e_c23, waits=[("pool", pl)], incs=[("dve", 1)])
            dv = bump("dve")
            c23d[cc] = dv
            op("act", lambda a, ccx=cc: (a.drain(), a.activation(
                sl2(ccx).bitcast(F32R), TA[:, :TC], AF.Silu))[1],
               waits=[("dve", dv)], incs=[("act", 1)])
            a_c = bump("act")
            silua[cc] = a_c
            for s in range(S + 1):
                g_ = cc * (S + 1) + s
                col = cc * S + min(s, S - 1)
                j = g_ % 2
                dstage = s < S

                def e_u(a, ccx=cc, colx=col, b=BQ[j], br=DB[j], d=dstage):
                    a.drain()
                    if d:
                        return a.activation(b, sl2(ccx), AF.Copy, bias=0.0,
                                            scale=cbt[:, colx:colx + 1])
                    return a.activation(br, sl2(ccx), AF.Copy, bias=0.0,
                                        scale=vt["dssm"][:, ccx:ccx + 1])
                uw = [("dve", dvescan[g_ - 2]), ("pe", ymm[g_ - 2])] \
                    if g_ >= 2 else [("act", a_c)]
                op("act", e_u, waits=uw, incs=[("act", 1)])
                a_u = bump("act")
                if dstage:
                    def e_s(v, b=BQ[j], hr=HB[j], colx=col):
                        return v.tensor_tensor_scan(
                            hr, abar[:, colx:colx + 1].broadcast_to([128, TC]),
                            b, zcol[:], ALU.mult, ALU.add)
                    sw = [("act", a_u)]
                    if g_ >= 2:
                        sw.append(("pe", ymm[g_ - 2]))
                    op("dve", e_s, waits=sw, incs=[("dve", 1)])
                    dvescan[g_] = bump("dve")

                    def e_x(a, h=HB[j], colx=col):
                        return a.activation(sloc[:, colx:colx + 1],
                                            h[:, TC - 1:TC], AF.Copy,
                                            bias=0.0, scale=1.0)
                    pend_x.append((e_x, dvescan[g_]))
                    if len(pend_x) > 1:
                        ex, exw = pend_x.pop(0)
                        op("act", ex, waits=[("dve", exw)], incs=[("act", 1)])
                        bump("act")
                else:
                    dvescan[g_] = cur("dve")

                def e_ym(pe, hr=(HB[j] if dstage else DB[j]), sx=s,
                         first=(s == 0), last=(s == S)):
                    mm = None
                    for blk in range(4):
                        mm = pe.matmul(ps[blk][:, :512], eye_b[:],
                                       hr[:, blk * 512:(blk + 1) * 512],
                                       start=first, stop=last)
                    return mm
                yw = [("dve", dvescan[g_])] if dstage else [("act", a_u)]
                if s == 0 and cc > 0:
                    yw.append(("act", yev[cc - 1]))
                if s == 0 and cc == 0:
                    yw.append(("act", a1b_act))
                op("pe", e_ym, waits=yw, incs=[("pe", 1)])
                ymm[g_] = bump("pe")
            while pend_x:
                ex, exw = pend_x.pop(0)
                op("act", ex, waits=[("dve", exw)], incs=[("act", 1)])
                bump("act")
            for blk in range(4):
                def e_ye(a, ccx=cc, b=blk):
                    a.drain()
                    return a.activation(
                        sl2(ccx)[:, b * 512:(b + 1) * 512].bitcast(F32R),
                        ps[b][:, :512], AF.Copy, bias=0.0, scale=1.0)
                op("act", e_ye, waits=[("pe", ymm[cc * (S + 1) + S])],
                   incs=[("act", 1)])
                yev[cc] = bump("act")
        if "xconv" in taps:
            for cc in range(CCN):
                dma(sl2(cc), tap_aps["xconv"][cc], sem="st",
                    waits=[("act", cur("act")), ("dve", cur("dve"))])
        if "sloc" in taps:
            dma(sloc[:], tap_aps["sloc"], sem="st", waits=[("dve", cur("dve"))])
        barrier()

        st1 = dma(sloc[:], cc1_in[:, :], sem="st")
        op("pool", lambda g: g.collective_compute(
            "AllGather", ALU.bypass, replica_groups=[list(range(8))],
            ins=[cc1_in.opt()], outs=[cc1_out.opt()]),
            waits=[("st", st1)], incs=[("cc", 1)])
        bump("cc")
        for i in range(8):
            li = dma(cc1_out[i * 128:(i + 1) * 128, :], TA[:, :128],
                     waits=allw() + [("cc", 1)])

            def e_co(g, ix=i):
                g.drain()
                return g.tensor_scalar(TB_[:, :128], lnat[:],
                                       selwt[:, ix:ix + 1], 0.0, ALU.mult,
                                       ALU.add)
            op("pool", e_co, waits=[("ld", li)], incs=[("pool", 1)])
            pl = bump("pool")
            op("act", lambda a: (a.drain(), a.activation(
                TCm[:, :128], TB_[:, :128], AF.Exp))[1],
               waits=[("pool", pl)], incs=[("act", 1)])
            ac = bump("act")

            def e_acc(v, ix=i):
                v.drain()
                v.tensor_scalar(TCm[:, 128:256], TCm[:, :128],
                                selmt[:, ix:ix + 1], None, ALU.mult)
                v.drain()
                v.tensor_tensor(TB_[:, :128], TA[:, :128], TCm[:, 128:256],
                                ALU.mult)
                v.drain()
                return v.tensor_tensor(hin[:], hin[:], TB_[:, :128], ALU.add)
            op("dve", e_acc, waits=[("act", ac)], incs=[("dve", 1)])
            bump("dve")
        if "hin" in taps:
            dma(hin[:], tap_aps["hin"], sem="st", waits=[("dve", cur("dve"))])
        barrier()

        op("pool", lambda g: (g.drain(), g.iota(
            AR1[:, 0:TC], pattern=[[1, TC]], base=1, channel_multiplier=0,
            allow_small_or_imprecise_dtypes=True))[1],
           waits=[("pe", cur("pe")), ("act", cur("act"))], incs=[("pool", 1)])
        iop = bump("pool")
        op("act", lambda a: (a.drain(), a.activation(
            abK[:], lnat[:], AF.Exp, bias=0.0, scale=float(TC // 2)))[1],
           incs=[("act", 1)])
        bump("act")
        corr_last = {}
        cmm = {}
        cevac = {}
        prod_dve = {}
        v1_ae = {}
        v1wl = {0: wslice("v1t", 0, WA, waits=[("pe", cur("pe"))])}
        for cc in range(CCN):
            for s in range(S):
                g2 = cc * S + s
                j = g2 % 2
                col = cc * S + s
                ew = [("pool", iop)]
                if g2 >= 2:
                    ew.append(("pe", cmm[g2 - 2]))

                def e_E(a, b=DB[j], colx=col):
                    a.drain()
                    return a.activation(b[:, 0:TC // 2], AR1[:, 0:TC // 2],
                                        AF.Exp, bias=0.0,
                                        scale=lnat[:, colx:colx + 1])
                op("act", e_E, waits=ew, incs=[("act", 1)])
                a_E = bump("act")

                def e_dg(v, b=dg[j], eb=DB[j], colx=col):
                    v.drain()
                    v.tensor_scalar(eb[:, TC // 2:TC], eb[:, 0:TC // 2],
                                    abK[:, colx:colx + 1], None, ALU.mult)
                    v.drain()
                    return v.tensor_scalar(b[:], eye[:],
                                           hin[:, colx:colx + 1],
                                           0.0, ALU.mult, ALU.add)
                dw = [("pe", cmm[g2 - 2])] if g2 >= 2 else []
                dw = dw + [("act", a_E)]
                op("dve", e_dg, waits=dw, incs=[("dve", 1)])
                d_g = bump("dve")

                def e_cm(pe, b=DB[j], d=dg[j], first=(s == 0),
                         last=(s == S - 1)):
                    mm = None
                    for blk in range(4):
                        mm = pe.matmul(ps[4 + blk][:, :512], d[:],
                                       b[:, blk * 512:(blk + 1) * 512],
                                       start=first, stop=last)
                    return mm
                cw = [("act", a_E), ("dve", d_g)]
                if s == 0 and cc >= 1:
                    cw.append(cevac[cc - 1])
                op("pe", e_cm, waits=cw, incs=[("pe", 1)])
                cmm[g2] = bump("pe")
            for blk in range(4):
                def e_ca(v, ccx=cc, blkx=blk, b=4 + blk):
                    v.drain()
                    return v.tensor_tensor(
                        sl2(ccx)[:, blkx * 512:(blkx + 1) * 512]
                        .bitcast(F32R),
                        sl2(ccx)[:, blkx * 512:(blkx + 1) * 512],
                        ps[b][:, :512], ALU.add)
                op("dve", e_ca, waits=[("pe", cmm[cc * S + S - 1])],
                   incs=[("dve", 1)])
                cevac[cc] = ("dve", bump("dve"))
            corr_last[cc] = cevac[cc]
            wbuf = WA if cc % 2 == 0 else WB
            if cc + 1 < CCN:
                v1wl[cc + 1] = wslice("v1t", cc + 1,
                                      WB if (cc + 1) % 2 == 1 else WA,
                                      waits=[("pe", cur("pe"))])
            for blk in range(4):
                g = cc * 4 + blk

                def e_mm(pe, w=wbuf, blkx=blk, b=blk):
                    mm = None
                    for k in range(CCN):
                        mm = pe.matmul(
                            ps[b][:, :512], w[:, k * 128:(k + 1) * 128],
                            sl1(k)[:, blkx * 512:(blkx + 1) * 512]
                            .bitcast(F32R),
                            start=(k == 0), stop=(k == CCN - 1))
                    return mm
                mw = [v1wl[cc]]
                if g >= 4:
                    mw.append(("act", v1_ae[g - 4]))
                op("pe", e_mm, waits=mw, incs=[("pe", 1)])
                pm = bump("pe")

                def e_ev(a, dcx=cc, b=blk, r=blk):
                    a.drain()
                    return a.activation(TB_[:, r * 512:r * 512 + 512],
                                        ps[b][:, :512], AF.Silu,
                                        bias=vt["b_v1"][:, dcx:dcx + 1],
                                        scale=1.0)
                ew2 = [("pe", pm)]
                if g >= 4:
                    ew2.append(("dve", prod_dve[g - 4]))
                op("act", e_ev, waits=ew2, incs=[("act", 1)])
                a_e = bump("act")
                v1_ae[g] = a_e

                def e_pr(v, dcx=cc, blkx=blk, r=blk):
                    v.drain()
                    return v.tensor_tensor(
                        sl2(dcx)[:, blkx * 512:(blkx + 1) * 512]
                        .bitcast(F32R),
                        sl2(dcx)[:, blkx * 512:(blkx + 1) * 512],
                        TB_[:, r * 512:r * 512 + 512], ALU.mult)
                op("dve", e_pr, waits=[("act", a_e), corr_last[cc]],
                   incs=[("dve", 1)])
                prod_dve[g] = bump("dve")
        if "prod" in taps:
            for cc in range(CCN):
                dma(sl2(cc), tap_aps["prod"][cc], sem="st",
                    waits=[("dve", cur("dve"))])
        barrier()

        w2wl = {0: wslice("w2t", 0, WA, waits=[("pe", cur("pe"))])}
        a_e3 = {}
        ng3 = CCN * NTB

        def b3a_group(g):
            dc, tb = divmod(g, NTB)
            wbuf = WA if dc % 2 == 0 else WB
            if tb == 0 and dc + 1 < CCN:
                w2wl[dc + 1] = wslice("w2t", dc + 1,
                                      WB if (dc + 1) % 2 == 1 else WA,
                                      waits=[("pe", cur("pe"))])
            waits = [w2wl[dc]]
            if g >= 2:
                waits.append(("act", a_e3[g - 2]))

            def e_mm(pe, w=wbuf, t=tb, b=6 + g % 2):
                mm = None
                for cc in range(CCN):
                    mm = pe.matmul(ps[b][:, :512], w[:, cc * 128:(cc + 1) * 128],
                                   sl2(cc)[:, t * 512:(t + 1) * 512].bitcast(F32R),
                                   start=(cc == 0), stop=(cc == CCN - 1))
                return mm
            op("pe", e_mm, waits=waits, incs=[("pe", 1)])
            pm = bump("pe")
            ew = [("pe", pm)]
            if g >= 2:
                ew.append(("st3r%d" % (g % 2), cur("st3r%d" % (g % 2))))

            def e_ev(a, dcx=dc, b=6 + g % 2, r=g % 2):
                a.drain()
                return a.activation(TCm[:, 1024 + r * 512:1024 + r * 512 + 512],
                                    ps[b][:, :512], AF.Identity,
                                    bias=vt["b_w2"][:, dcx:dcx + 1], scale=1.0)
            op("act", e_ev, waits=ew, incs=[("act", 1)])
            a_e3[g] = bump("act")
            dma(TCm[:, 1024 + (g % 2) * 512:1024 + (g % 2) * 512 + 512],
                at0_h[dc, :, tb * 512:(tb + 1) * 512], sem="st3r%d" % (g % 2),
                waits=[("act", a_e3[g])])

        g3 = 0
        for ts in range(NTS):
            for cc in range(CCN):
                bnk, off = cc % 2, (cc // 2 % 2) * 128

                def e_tf(g, ccx=cc, t=ts, b=bnk, o=off):
                    return g.transpose(ps[b][:, o:o + 128],
                                       sl1(ccx)[:, t * 128:(t + 1) * 128],
                                       eye[:])
                op("pe", e_tf, waits=[("dve", cur("dve")), ("act", cur("act"))],
                   incs=[("pe", 1)])
                tp = bump("pe")

                def e_ev(a, ccx=cc, b=bnk, o=off):
                    a.drain()
                    return a.activation(TA[:, ccx * 128:(ccx + 1) * 128],
                                        ps[b][:, o:o + 128], AF.Copy,
                                        bias=0.0, scale=1.0)
                op("act", e_ev, waits=[("pe", tp)], incs=[("act", 1)])
                bump("act")
            q = emit_quant(TA[:, :1024], TB_[:, :1024], TCm[:, :1024],
                           TA[:, 1024:2048], TB_[:, 1024:2048],
                           waits=[("act", cur("act"))])
            for cc in range(CCN):
                bnk, off = 4 + cc % 2, (cc // 2 % 2) * 128

                def e_tb(g, ccx=cc, b=bnk, o=off):
                    return g.transpose(ps[b][:, o:o + 128],
                                       TB_[:, ccx * 128:(ccx + 1) * 128],
                                       eye[:])
                op("pe", e_tb, waits=[("dve", q), ("act", cur("act"))],
                   incs=[("pe", 1)])
                tp = bump("pe")

                def e_ev2(a, ccx=cc, t=ts, b=bnk, o=off):
                    a.drain()
                    return a.activation(sl1(ccx)[:, t * 128:(t + 1) * 128]
                                        .bitcast(F32R),
                                        ps[b][:, o:o + 128], AF.Copy,
                                        bias=0.0, scale=1.0)
                op("act", e_ev2, waits=[("pe", tp)], incs=[("act", 1)])
                bump("act")
            while g3 < ng3 * (ts + 1) // NTS:
                b3a_group(g3)
                g3 += 1
        while g3 < ng3:
            b3a_group(g3)
            g3 += 1
        if "xd" in taps:
            for cc in range(CCN):
                dma(sl1(cc), tap_aps["xd"][cc], sem="st",
                    waits=[("act", cur("act"))])
        barrier()

        f0wl = {0: wslice("wdt", 0, WA, waits=[("pe", cur("pe"))])}
        tr_pe = {}
        sum_dve = {}
        b3b_ae = {}
        for g in range(ng3):
            dc, tb = divmod(g, NTB)
            wbuf = WA if dc % 2 == 0 else WB
            if tb == 0 and dc + 1 < CCN:
                f0wl[dc + 1] = wslice("wdt", dc + 1,
                                      WB if (dc + 1) % 2 == 1 else WA,
                                      waits=[("pe", cur("pe"))])
            xl = [("dve", sum_dve[g - 2])] if g >= 2 else [("dve", cur("dve"))]
            xl4 = dma(xfh_in[dc, :, 1 + tb * 512:1 + (tb + 1) * 512],
                      TCm[:, (g % 2) * 512:(g % 2) * 512 + 512],
                      sem="ldx", waits=xl)
            aw = [("dve", sum_dve[g - 4])] if g >= 4 else \
                [("st3r0", cur("st3r0")), ("st3r1", cur("st3r1"))]
            al = dma(at0_h[dc, :, tb * 512:(tb + 1) * 512],
                     TB_[:, (g % 4) * 512:(g % 4) * 512 + 512], sem="lda",
                     waits=aw)
            mmw = [f0wl[dc], ("ldx", cur("ldx"))]
            if g >= 4:
                mmw.append(("act", b3b_ae[g - 4]))

            def e_mm(pe, w=wbuf, t=tb, b=2 + g % 4):
                mm = None
                for cc in range(CCN):
                    mm = pe.matmul(ps[b][:, :512],
                                   w[:, cc * 128:(cc + 1) * 128],
                                   sl1(cc)[:, t * 512:(t + 1) * 512]
                                   .bitcast(F32R),
                                   start=(cc == 0), stop=(cc == CCN - 1))
                return mm
            op("pe", e_mm, waits=mmw, incs=[("pe", 1)])
            pm = bump("pe")
            tr_pe[g] = pm
            ew = [("pe", pm)]
            if g >= 4:
                ew.append(("st4r%d" % (g % 4), cur("st4r%d" % (g % 4))))

            def e_ev(a, dcx=dc, b=2 + g % 4, r=g % 4):
                a.drain()
                return a.activation(TA[:, r * 512:r * 512 + 512],
                                    ps[b][:, :512], AF.Identity,
                                    bias=vt["b_f"][:, dcx:dcx + 1], scale=1.0)
            op("act", e_ev, waits=ew, incs=[("act", 1)])
            b3b_ae[g] = bump("act")
            a_e = ("act", b3b_ae[g])

            def e_sum(v, r=g % 4, b=g % 2):
                v.drain()
                v.tensor_tensor(TA[:, r * 512:r * 512 + 512],
                                TA[:, r * 512:r * 512 + 512],
                                TB_[:, r * 512:r * 512 + 512], ALU.add)
                v.drain()
                return v.tensor_tensor(TA[:, r * 512:r * 512 + 512],
                                       TA[:, r * 512:r * 512 + 512],
                                       TCm[:, b * 512:b * 512 + 512], ALU.add)
            op("dve", e_sum, waits=[a_e, ("lda", al), ("ldx", cur("ldx"))],
               incs=[("dve", 1)])
            sum_dve[g] = bump("dve")
            dma(TA[:, (g % 4) * 512:(g % 4) * 512 + 512],
                x1f_h[dc, :, tb * 512:(tb + 1) * 512], sem="st4r%d" % (g % 4),
                waits=[("dve", sum_dve[g])])
        if "x1r" in taps:
            for cc in range(CCN):
                dma(x1f_h[cc], tap_aps["x1r"][cc], sem="st",
                    waits=[("st4r%d" % r, cur("st4r%d" % r))
                           for r in range(4)])
        barrier()

        for ck in range(NCK):
            c0 = ck * 256
            lds = allw() + [("st4r%d" % r, cur("st4r%d" % r))
                            for r in range(4)]
            for cc in range(CCN):
                dma(x1f_h[cc, :, c0:c0 + 256], AR1[:, cc * 256:(cc + 1) * 256],
                    sem="ld", waits=lds)
                lds = ()
            lA = cur("ld")

            def e_sq(a):
                r = None
                for cc in range(CCN):
                    a.drain()
                    r = a.activation(AR2[:, cc * 256:(cc + 1) * 256],
                                     AR1[:, cc * 256:(cc + 1) * 256], AF.Square)
                return r
            op("act", e_sq, waits=[("ld", lA)], incs=[("act", 1)])
            a_sq = bump("act")

            def e_st(g):
                for cc in range(CCN):
                    g.matmul(ps[4][:, :256], ones_ln[:],
                             AR1[:, cc * 256:(cc + 1) * 256],
                             start=(cc == 0), stop=(cc == CCN - 1))
                mm = None
                for cc in range(CCN):
                    mm = g.matmul(ps[5][:, :256], ones_ln[:],
                                  AR2[:, cc * 256:(cc + 1) * 256],
                                  start=(cc == 0), stop=(cc == CCN - 1))
                return mm
            op("pe", e_st, waits=[("act", a_sq), ("ld", lA)], incs=[("pe", 1)])
            p_st = bump("pe")

            def e_m2(a):
                a.drain()
                return a.activation(stat1[:, :256], ps[4][:, :256], AF.Square)
            op("act", e_m2, waits=[("pe", p_st)], incs=[("act", 1)])
            a_m2 = bump("act")

            def e_var(v):
                v.drain()
                return v.tensor_tensor(stat2[:, :256], ps[5][:, :256],
                                       stat1[:, :256], ALU.subtract)
            op("dve", e_var, waits=[("act", a_m2)], incs=[("dve", 1)])
            d_v = bump("dve")
            op("act", lambda a: (a.drain(), a.activation(
                stat1[:, :256], stat2[:, :256], AF.Sqrt, bias=epsc[:],
                scale=1.0))[1], waits=[("dve", d_v)], incs=[("act", 1)])
            a_sd = bump("act")
            op("dve", lambda v: (v.drain(), v.reciprocal(
                stat2[:, :256], stat1[:, :256]))[1],
               waits=[("act", a_sd)], incs=[("dve", 1)])
            d_r = bump("dve")
            for cc in range(CCN):
                def e_n(v, ccx=cc, c0x=c0):
                    v.drain()
                    v.tensor_tensor(AR2[:, ccx * 256:(ccx + 1) * 256],
                                    AR1[:, ccx * 256:(ccx + 1) * 256],
                                    ps[4][:, :256], ALU.subtract)
                    v.drain()
                    return v.tensor_tensor(sl1(ccx)[:, c0x:c0x + 256]
                                           .bitcast(F32R),
                                           AR2[:, ccx * 256:(ccx + 1) * 256],
                                           stat2[:, :256], ALU.mult)
                op("dve", e_n, waits=[("dve", d_r)], incs=[("dve", 1)])
                bump("dve")
        if "xn1" in taps:
            for cc in range(CCN):
                dma(sl1(cc), tap_aps["xn1"][cc], sem="st",
                    waits=[("dve", cur("dve"))])
        barrier()

        for (wn, bn, fn, hap, stn) in [("wkt", "b_k", AF.Relu, kf_h, "stk"),
                                       ("wvt", "b_v", AF.Identity, vf_h, "stv")]:
            def post_kv(dc, blk, bank, g, fnx=fn, bnx=bn, hapx=hap, stx=stn):
                reg = g % 4
                sreg = stx + "r%d" % reg
                pw = [("pe", cur("pe"))]
                pw.append((sreg, cur(sreg)))
                if stx == "stv" and g < 4:
                    pw.append(("stkr%d" % reg, cur("stkr%d" % reg)))

                def e_ev(a, dcx=dc, b=bank, r=reg):
                    a.drain()
                    return a.activation(TA[:, r * 512:r * 512 + 512],
                                        ps[b][:, :512], fnx,
                                        bias=vt[bnx][:, dcx:dcx + 1], scale=1.0)
                op("act", e_ev, waits=pw, incs=[("act", 1)])
                a_e = bump("act")
                dma(TA[:, reg * 512:reg * 512 + 512],
                    hapx[dc, :, blk * 512:(blk + 1) * 512], sem=sreg,
                    waits=[("act", a_e)])
                return ("act", a_e)
            proj_pipe(wn, lambda cc, blk: sl1(cc)[:, blk * 512:(blk + 1) * 512],
                      post_kv, NTB, 512)
        barrier()

        op("pool", lambda g: (g.memset(kvp, 0.0), g.memset(kslo[:], 0.0),
                              g.drain())[2], waits=allw(), incs=[("pool", 1)])
        bump("pool")
        for ts in range(NTS):
            lk = allw() + [("stkr%d" % r, cur("stkr%d" % r))
                           for r in range(4)] + \
                 [("stvr%d" % r, cur("stvr%d" % r)) for r in range(4)] + \
                 [("pool", cur("pool"))]
            for cc in range(CCN):
                dma(kf_h[cc, :, ts * 128:(ts + 1) * 128],
                    TCm[:, cc * 128:(cc + 1) * 128], sem="ld", waits=lk)
                lk = ()
                dma(vf_h[cc, :, ts * 128:(ts + 1) * 128],
                    TCm[:, 1024 + cc * 128:1024 + (cc + 1) * 128], sem="ld")
            lA = cur("ld")
            for j, dst in [(0, TA), (1, TB_)]:
                for cc in range(CCN):
                    bnk, off = j, (cc % 4) * 128

                    def e_tr(g, ccx=cc, jx=j, b=bnk, o=off):
                        return g.transpose(ps[b][:, o:o + 128],
                                           TCm[:, jx * 1024 + ccx * 128:
                                               jx * 1024 + (ccx + 1) * 128],
                                           eye[:])
                    op("pe", e_tr,
                       waits=[("ld", lA), ("act", cur("act"))] if cc == 0
                       else [("act", cur("act"))], incs=[("pe", 1)])
                    tp = bump("pe")

                    def e_ev(a, ccx=cc, d=dst, b=bnk, o=off):
                        a.drain()
                        return a.activation(d[:, ccx * 128:(ccx + 1) * 128],
                                            ps[b][:, o:o + 128], AF.Copy,
                                            bias=0.0, scale=1.0)
                    op("act", e_ev, waits=[("pe", tp)], incs=[("act", 1)])
                    bump("act")
            a_tm = cur("act")

            def e_kv(g, t=ts):
                mm = None
                for h in range(H):
                    r0 = 64 * (h % 2)
                    mm = g.matmul(ps[6][r0:r0 + 64, (h // 2) * 64:
                                        (h // 2) * 64 + 64],
                                  TA[:, h * 64:(h + 1) * 64],
                                  TB_[:, h * 64:(h + 1) * 64],
                                  start=True, stop=True, skip_group_check=True)
                for h in range(H):
                    r0 = 64 * (h % 2)
                    mm = g.matmul(ps[7][r0:r0 + 64, (h // 2):(h // 2) + 1],
                                  TA[:, h * 64:(h + 1) * 64], onescol[:],
                                  start=True, stop=True, skip_group_check=True)
                return mm
            op("pe", e_kv, waits=[("act", a_tm), ("dve", cur("dve"))],
               incs=[("pe", 1)])
            pkv = bump("pe")

            def e_kacc(v):
                v.drain()
                v.tensor_tensor(kvp, kvp, ps[6][:, :512], ALU.add)
                v.drain()
                return v.tensor_tensor(kslo[:], kslo[:], ps[7][:, :8], ALU.add)
            op("dve", e_kacc, waits=[("pe", pkv)], incs=[("dve", 1)])
            bump("dve")
        def e_evkv(v):
            v.drain()
            v.tensor_copy(pack[:, 0:512], kvp)
            v.drain()
            v.tensor_copy(pack[:, 512:520], kslo[:])
            v.drain()
            r = None
            for cc in range(CCN):
                r = v.tensor_copy(pack[:, 520 + cc:521 + cc],
                                  sl1(cc)[:, TC - 1:TC])
                v.drain()
            return r
        op("dve", e_evkv, waits=[("pe", cur("pe"))], incs=[("dve", 1)])
        dpk = bump("dve")
        st2 = dma(pack, cc2_in[:, :], sem="st", waits=[("dve", dpk)])
        op("pool", lambda g: g.collective_compute(
            "AllGather", ALU.bypass, replica_groups=[list(range(8))],
            ins=[cc2_in.opt()], outs=[cc2_out.opt()]),
            waits=[("st", st2)], incs=[("cc2", 1)])
        bump("cc2")
        op("pool", lambda g: (g.memset(kvg[:], 0.0), g.memset(ksg[:], 0.0),
                              g.drain(), g.memset(xprev[:], 0.0))[3],
           waits=[("dve", cur("dve"))], incs=[("pool", 1)])
        bump("pool")
        for i in range(8):
            li = dma(cc2_out[i * 128:(i + 1) * 128, :], TA[:, :528],
                     waits=allw() + [("cc2", 1)])

            def e_red(v, ix=i):
                v.drain()
                v.scalar_tensor_tensor(kvg[:], TA[:, 0:512],
                                       kvselt[:, ix:ix + 1], kvg[:],
                                       ALU.mult, ALU.add)
                v.drain()
                v.scalar_tensor_tensor(ksg[:], TA[:, 512:520],
                                       kvselt[:, ix:ix + 1], ksg[:],
                                       ALU.mult, ALU.add)
                v.drain()
                return v.scalar_tensor_tensor(xprev[:], TA[:, 520:528],
                                              pvselt[:, ix:ix + 1], xprev[:],
                                              ALU.mult, ALU.add)
            op("dve", e_red, waits=[("ld", li)], incs=[("dve", 1)])
            bump("dve")

        def e_bd(v):
            r = None
            for dcc in range(CCN):
                v.drain()
                v.tensor_copy(ksbd[0:64, dcc * 16 + 2 * dcc:
                                   dcc * 16 + 2 * dcc + 1],
                              ksg[0:64, dcc:dcc + 1])
                v.drain()
                r = v.tensor_copy(ksbd[64:128, dcc * 16 + 2 * dcc + 1:
                                       dcc * 16 + 2 * dcc + 2],
                                  ksg[64:128, dcc:dcc + 1])
            return r
        op("dve", e_bd, waits=[("dve", cur("dve")), ("pool", cur("pool"))],
           incs=[("dve", 1)])
        bump("dve")
        op("pool", lambda g: (g.memset(kvbd, 0.0), g.drain())[1],
           waits=[("dve", cur("dve"))], incs=[("pool", 1)])
        bump("pool")

        def e_kvbd(v):
            r = None
            for j in range(CCN):
                v.drain()
                v.tensor_copy(kvbd[0:64, j * 128:j * 128 + 64],
                              kvg[0:64, j * 64:(j + 1) * 64])
                v.drain()
                r = v.tensor_copy(kvbd[64:128, j * 128 + 64:j * 128 + 128],
                                  kvg[64:128, j * 64:(j + 1) * 64])
            return r
        op("dve", e_kvbd, waits=[("pool", cur("pool"))], incs=[("dve", 1)])
        bump("dve")
        if "kvg" in taps:
            def e_tapkv(v):
                v.drain()
                v.tensor_copy(pack[:, 0:512], kvg[:])
                v.drain()
                v.tensor_copy(pack[:, 512:520], ksg[:])
                v.drain()
                return v.tensor_copy(pack[:, 520:528], xprev[:])
            op("dve", e_tapkv, waits=[("dve", cur("dve"))], incs=[("dve", 1)])
            bump("dve")
            dma(pack, tap_aps["kvg"], sem="st", waits=[("dve", cur("dve"))])
        barrier()

        def post_q(dc, blk, bank, g):
            def e_ev(a, dcx=dc, b=bank, blkx=blk):
                a.drain()
                return a.activation(sl2(dcx)[:, blkx * 512:(blkx + 1) * 512]
                                    .bitcast(F32R),
                                    ps[b][:, :512], AF.Relu,
                                    bias=vt["b_q"][:, dcx:dcx + 1], scale=1.0)
            op("act", e_ev, waits=[("pe", cur("pe"))], incs=[("act", 1)])
            return ("act", bump("act"))
        proj_pipe("wqt", lambda cc, blk: sl1(cc)[:, blk * 512:(blk + 1) * 512],
                  post_q, NTB, 512)
        if "qf" in taps:
            for cc in range(CCN):
                dma(sl2(cc), tap_aps["qf"][cc], sem="st",
                    waits=[("act", cur("act"))])
        barrier()
        gld = dma(grow_in, grow, waits=allw())
        gld = dma(brow_in, brow)
        for half in range(2):
            def e_g(g, h=half):
                g.matmul(ps[0][:, :512], ones1[:],
                         grow[:, h * 512:(h + 1) * 512], start=True, stop=True)
                return g.matmul(ps[1][:, :512], ones1[:],
                                brow[:, h * 512:(h + 1) * 512],
                                start=True, stop=True)
            op("pe", e_g, waits=[("pe", cur("pe")), ("dve", cur("dve")),
                                 ("ld", gld)],
               incs=[("pe", 1)])
            pb = bump("pe")

            def e_gev(v, h=half):
                v.drain()
                v.tensor_copy(gbc[:, h * 512:(h + 1) * 512], ps[0][:, :512])
                v.drain()
                return v.tensor_copy(bbc[:, h * 512:(h + 1) * 512],
                                     ps[1][:, :512])
            op("dve", e_gev, waits=[("pe", pb)], incs=[("dve", 1)])
            bump("dve")
        for ts in range(NTS):
            def e_dn(g, t=ts):
                mm = None
                for cc in range(CCN):
                    mm = g.matmul(ps[2][:, :16],
                                  sl2(cc)[:, t * 128:(t + 1) * 128],
                                  ksbd[:, cc * 16:(cc + 1) * 16],
                                  start=(cc == 0), stop=(cc == CCN - 1))
                return mm
            op("pe", e_dn, waits=allw() if ts == 0 else [("dve", cur("dve"))],
               incs=[("pe", 1)])
            pd = bump("pe")

            def e_rd(v, t=ts):
                v.drain()
                v.tensor_scalar(stat1[:, :16], ps[2][:, :16], EPS_DEN, None,
                                ALU.max)
                v.drain()
                return v.reciprocal(rden[:, t * 16:(t + 1) * 16],
                                    stat1[:, :16])
            op("dve", e_rd, waits=[("pe", pd)], incs=[("dve", 1)])
            drd = bump("dve")

            def e_cx(g, t=ts):
                mm = None
                for j in range(CCN):
                    bnk = 3 + j // 4
                    mm = g.matmul(ps[bnk][:, (j % 4) * 128:(j % 4) * 128 + 128],
                                  sl2(j)[:, t * 128:(t + 1) * 128],
                                  kvbd[:, j * 128:(j + 1) * 128],
                                  start=True, stop=True, skip_group_check=True)
                return mm
            op("pe", e_cx, waits=[("dve", drd)], incs=[("pe", 1)])
            pc = bump("pe")

            def e_tr(g, t=ts):
                g.drain()
                mm = None
                for cc in range(CCN):
                    b, o = cc // 4, (cc % 4) * 128
                    mm = g.transpose(ps[b][:, o:o + 128],
                                     sl1(cc)[:, t * 128:(t + 1) * 128], eye[:])
                return mm
            op("pe", e_tr, incs=[("pe", 1)])
            ptr = bump("pe")

            def e_ep(v, t=ts):
                v.drain()
                for h in range(H):
                    bnk = 3 + h // 8
                    v.tensor_scalar(TA[:, h * 64:(h + 1) * 64],
                                    ps[bnk][:, (h % 8) * 64:(h % 8) * 64 + 64],
                                    rden[:, t * 16 + h:t * 16 + h + 1], None,
                                    ALU.mult)
                v.drain()
                v.tensor_tensor(TA[:, 0:512], TA[:, 0:512], ps[0][:, :512],
                                ALU.add)
                v.drain()
                v.tensor_tensor(TA[:, 512:1024], TA[:, 512:1024],
                                ps[1][:, :512], ALU.add)
                v.drain()
                v.tensor_reduce(stat1[:, 0:1], TA[:, :1024], axis=AX.X,
                                op=ALU.add)
                v.drain()
                return v.tensor_scalar(stat1[:, 1:2], stat1[:, 0:1],
                                       1.0 / 1024.0, None, ALU.mult)
            op("dve", e_ep, waits=[("pe", pc), ("pe", ptr), ("act", cur("act")),
                                   ("sta", cur("sta"))],
               incs=[("dve", 1)])
            dep = bump("dve")
            op("act", lambda a: (a.drain(), a.activation(
                TB_[:, :1024], TA[:, :1024], AF.Square))[1],
               waits=[("dve", dep)], incs=[("act", 1)])
            asq = bump("act")

            def e_st3(v):
                v.drain()
                v.tensor_reduce(stat1[:, 2:3], TB_[:, :1024], axis=AX.X,
                                op=ALU.add)
                v.drain()
                v.tensor_scalar(stat1[:, 3:4], stat1[:, 2:3], 1.0 / 1024.0,
                                None, ALU.mult)
                v.drain()
                v.tensor_tensor(stat1[:, 4:5], stat1[:, 1:2], stat1[:, 1:2],
                                ALU.mult)
                v.drain()
                return v.tensor_tensor(stat1[:, 5:6], stat1[:, 3:4],
                                       stat1[:, 4:5], ALU.subtract)
            op("dve", e_st3, waits=[("act", asq)], incs=[("dve", 1)])
            dst3 = bump("dve")
            op("act", lambda a: (a.drain(), a.activation(
                stat1[:, 6:7], stat1[:, 5:6], AF.Sqrt, bias=epsc[:],
                scale=1.0))[1], waits=[("dve", dst3)], incs=[("act", 1)])
            asd = bump("act")

            def e_fin(v, t=ts):
                v.drain()
                v.reciprocal(stat1[:, 7:8], stat1[:, 6:7])
                v.drain()
                v.tensor_scalar(TA[:, :1024], TA[:, :1024], stat1[:, 1:2],
                                stat1[:, 7:8], ALU.subtract, ALU.mult)
                v.drain()
                v.tensor_tensor(TA[:, :1024], TA[:, :1024], gbc[:], ALU.mult)
                v.drain()
                return v.tensor_tensor(TA[:, :1024], TA[:, :1024], bbc[:],
                                       ALU.add)
            op("dve", e_fin, waits=[("act", asd)], incs=[("dve", 1)])
            dfin = bump("dve")
            dma(TA[:, :1024], attn_h[ts * 128:(ts + 1) * 128, :], sem="sta",
                waits=[("dve", dfin)])
        if "att" in taps:
            dma(attn_h[:, :], tap_aps["att"], sem="st",
                waits=[("sta", cur("sta"))])
        barrier()

        for cc in range(CCN):
            def e_mix(v, ccx=cc):
                v.drain()
                v.tensor_tensor(TB_[:, 1:TC], sl1(ccx)[:, 1:TC],
                                sl1(ccx)[:, 0:TC - 1], ALU.subtract)
                v.drain()
                v.tensor_tensor(TB_[:, 0:1], sl1(ccx)[:, 0:1],
                                xprev[:, ccx:ccx + 1], ALU.subtract)
                v.drain()
                v.scalar_tensor_tensor(sl2(ccx)[:, 1:TC].bitcast(F32R),
                                       TB_[:, 1:TC],
                                       vt["mu_k"][:, ccx:ccx + 1],
                                       sl1(ccx)[:, 0:TC - 1], ALU.mult, ALU.add)
                v.drain()
                v.scalar_tensor_tensor(sl2(ccx)[:, 0:1].bitcast(F32R),
                                       TB_[:, 0:1],
                                       vt["mu_k"][:, ccx:ccx + 1],
                                       xprev[:, ccx:ccx + 1], ALU.mult, ALU.add)
                v.drain()
                v.scalar_tensor_tensor(TA[:, 1:TC], TB_[:, 1:TC],
                                       vt["mu_r"][:, ccx:ccx + 1],
                                       sl1(ccx)[:, 0:TC - 1], ALU.mult, ALU.add)
                v.drain()
                v.scalar_tensor_tensor(TA[:, 0:1], TB_[:, 0:1],
                                       vt["mu_r"][:, ccx:ccx + 1],
                                       xprev[:, ccx:ccx + 1], ALU.mult, ALU.add)
                v.drain()
                return v.tensor_copy(sl1(ccx).bitcast(F32R), TA[:, :TC])
            op("dve", e_mix, waits=allw() if cc == 0 else (), incs=[("dve", 1)])
            bump("dve")
        dmx = cur("dve")

        def post_r(dc, blk, bank, g):
            reg = g % 4
            sreg = "strr%d" % reg
            pw = [("pe", cur("pe"))]
            if g >= 4:
                pw.append((sreg, cur(sreg)))

            def e_ev(a, b=bank, r=reg):
                a.drain()
                return a.activation(TA[:, r * 512:r * 512 + 512],
                                    ps[b][:, :512], AF.Sigmoid)
            op("act", e_ev, waits=pw, incs=[("act", 1)])
            a_e = bump("act")
            dma(TA[:, reg * 512:reg * 512 + 512],
                r_h[dc, :, blk * 512:(blk + 1) * 512], sem=sreg,
                waits=[("act", a_e)])
            return ("act", a_e)
        proj_pipe("wrt", lambda cc, blk: sl1(cc)[:, blk * 512:(blk + 1) * 512],
                  post_r, NTB, 512, movwaits=[("dve", dmx)])
        barrier()

        ka_dve = {}

        ka_dve2 = {}

        def post_ka(dc, blk, bank, g):
            reg = g % 4
            pw = [("pe", cur("pe"))]
            if g >= 4:
                pw.append(("dve", ka_dve2[g - 4]))

            def e_ev(a, b=bank, r=reg):
                a.drain()
                return a.activation(TB_[:, r * 512:r * 512 + 512],
                                    ps[b][:, :512], AF.Relu)
            op("act", e_ev, waits=pw, incs=[("act", 1)])
            a_e = bump("act")

            def e_sqk(v, dcx=dc, blkx=blk, r=reg):
                v.drain()
                return v.tensor_tensor(
                    sl1(dcx)[:, blkx * 512:(blkx + 1) * 512].bitcast(F32R),
                    TB_[:, r * 512:r * 512 + 512],
                    TB_[:, r * 512:r * 512 + 512], ALU.mult)
            op("dve", e_sqk, waits=[("act", a_e)], incs=[("dve", 1)])
            ka_dve2[g] = bump("dve")
            return ("act", a_e)
        proj_pipe("wkkt", lambda cc, blk: sl2(cc)[:, blk * 512:(blk + 1) * 512],
                  post_ka, NTB, 512)
        barrier()

        o_dve = {}

        def post_o(dc, blk, bank, g):
            reg = g % 4
            lw = [("dve", cur("dve"))] if g < 4 else [("dve", o_dve[g - 4])]
            if g == 0:
                lw += [("strr%d" % r, cur("strr%d" % r)) for r in range(4)]
            lr = dma(r_h[dc, :, blk * 512:(blk + 1) * 512],
                     TB_[:, reg * 512:reg * 512 + 512], sem="ldr%d" % reg,
                     waits=lw)

            def e_o(v, dcx=dc, blkx=blk, b=bank, r=reg):
                v.drain()
                return v.tensor_tensor(
                    sl2(dcx)[:, blkx * 512:(blkx + 1) * 512].bitcast(F32R),
                    TB_[:, r * 512:r * 512 + 512], ps[b][:, :512], ALU.mult)
            op("dve", e_o, waits=[("pe", cur("pe")), ("ldr%d" % reg, lr)],
               incs=[("dve", 1)])
            o_dve[g] = bump("dve")
            return ("dve", o_dve[g])
        proj_pipe("wvvt", lambda cc, blk: sl1(cc)[:, blk * 512:(blk + 1) * 512],
                  post_o, NTB, 512)
        if "rwkv" in taps:
            for cc in range(CCN):
                dma(sl2(cc), tap_aps["rwkv"][cc], sem="st",
                    waits=[("dve", cur("dve"))])
        barrier()

        tr6 = {}
        f6 = {}
        for ts in range(NTS):
            lw = allw() + [("st4r%d" % r, cur("st4r%d" % r))
                           for r in range(4)]
            if ts >= 1:
                lw = [("pe", tr6[ts - 1])]
            for cc in range(CCN):
                dma(x1f_h[cc, :, ts * 128:(ts + 1) * 128],
                    TCm[:, cc * 128:(cc + 1) * 128], sem="ld6x", waits=lw)
                lw = ()
            l6 = cur("ld6x")
            la = dma(attn_h[ts * 128:(ts + 1) * 128, :], TA[:, :1024],
                     sem="ld6a", waits=[("out", cur("out"))])

            def e_tr6(g, t=ts):
                g.drain()
                mm = None
                for cc in range(CCN):
                    b, o = cc // 4, (cc % 4) * 128
                    mm = g.transpose(ps[b][:, o:o + 128],
                                     TCm[:, cc * 128:(cc + 1) * 128], eye[:])
                for cc in range(CCN):
                    b, o = 2 + cc // 4, (cc % 4) * 128
                    mm = g.transpose(ps[b][:, o:o + 128],
                                     sl2(cc)[:, t * 128:(t + 1) * 128], eye[:])
                return mm
            w6 = [("ld6x", l6)]
            if ts == 0:
                w6 += allw()
            else:
                w6.append(("dve", f6[ts - 1]))
            op("pe", e_tr6, waits=w6, incs=[("pe", 1)])
            tr6[ts] = bump("pe")

            def e_f(v):
                v.drain()
                v.tensor_tensor(TA[:, 0:512], TA[:, 0:512], ps[0][:, :512],
                                ALU.add)
                v.drain()
                v.tensor_tensor(TA[:, 512:1024], TA[:, 512:1024],
                                ps[1][:, :512], ALU.add)
                v.drain()
                v.tensor_tensor(TA[:, 0:512], TA[:, 0:512], ps[2][:, :512],
                                ALU.add)
                v.drain()
                return v.tensor_tensor(TA[:, 512:1024], TA[:, 512:1024],
                                       ps[3][:, :512], ALU.add)
            op("dve", e_f, waits=[("pe", tr6[ts]), ("ld6a", la)],
               incs=[("dve", 1)])
            f6[ts] = bump("dve")
            dma(TA[:, :1024], out[ts * 128:(ts + 1) * 128, :], sem="out",
                waits=[("dve", f6[ts])])
        op("sp", None, waits=[("out", cur("out"))] +
           [(s, cur(s)) for s in cnt if s.startswith("st")])

        if upto is not None:
            del ops[stage_marks[upto]:]
        sems = {}
        for (eng, waits, emit, incs) in ops:
            for (s, v) in waits:
                if s not in sems:
                    sems[s] = es.enter_context(nc.semaphore(s))
            for (s, n) in incs:
                if s not in sems:
                    sems[s] = es.enter_context(nc.semaphore(s))

        block = es.enter_context(nc.Block())

        def replay(role, eng):
            for (e, waits, emit, incs) in ops:
                if e != role:
                    continue
                for (s, v) in waits:
                    if v > 0:
                        eng.wait_ge(sems[s], v)
                instr = emit(eng) if emit is not None else None
                for (s, n) in incs:
                    assert instr is not None
                    instr.then_inc(sems[s], n)

        @block.tensor
        def _(pe):
            replay("pe", pe)

        @block.vector
        def _(dve):
            replay("dve", dve)

        @block.scalar
        def _(act):
            replay("act", act)

        @block.gpsimd
        def _(pool):
            replay("pool", pool)

        @block.sync
        def _(sp):
            replay("sp", sp)

    return nc, tap_aps



def _preslice(wt):
    return np.ascontiguousarray(
        wt.reshape(CCN, 128, CCN, 128).transpose(2, 0, 1, 3))


def _host_inputs(inputs, TC, core):
    b, j = core
    x = np.asarray(inputs["L0_attn_x"])
    B, L, _ = x.shape
    t0 = j * TC
    xs = np.zeros((TC + 2, C), np.float32)
    xs[1:TC + 1] = x[b, t0:t0 + TC]
    if t0 > 0:
        xs[0] = x[b, t0 - 1]
    if t0 + TC < L:
        xs[TC + 1] = x[b, t0 + TC]
    T = lambda w: _preslice(np.ascontiguousarray(np.asarray(w, np.float32).T))
    nj = L // TC
    selw = np.zeros(8, np.float32)
    selm = np.zeros(8, np.float32)
    for i in range(nj):
        gi = b * nj + i
        if i < j:
            selm[gi] = 1.0
            selw[gi] = float(TC * (j - 1 - i))
    kvsel = np.zeros(8, np.float32)
    for i in range(nj):
        kvsel[b * nj + i] = 1.0
    pvsel = np.zeros(8, np.float32)
    if j > 0:
        pvsel[b * nj + j - 1] = 1.0
    hmask = np.array([0.0 if j == 0 else 1.0,
                      0.0 if j == nj - 1 else 1.0], np.float32)
    rep = lambda v: np.ascontiguousarray(
        np.broadcast_to(np.asarray(v, np.float32)[None, :], (128, len(v))))
    return {
        "xfh": np.ascontiguousarray(xs.T.reshape(CCN, 128, TC + 2)),
        "w1t": T(inputs["L0_attn_w1_w"]), "v1t": T(inputs["L0_attn_v1_w"]),
        "w2t": T(inputs["L0_attn_w2_w"]),
        "wd": np.ascontiguousarray(np.asarray(inputs["L0_ffn_weight"],
                                              np.float32)),
        "wqt": T(inputs["L1_attn_Wq"]), "wkt": T(inputs["L1_attn_Wk"]),
        "wvt": T(inputs["L1_attn_Wv"]), "wrt": T(inputs["L1_ffn_W_r"]),
        "wkkt": T(inputs["L1_ffn_W_k"]), "wvvt": T(inputs["L1_ffn_W_v"]),
        "b_w1": np.asarray(inputs["L0_attn_w1_b"], np.float32),
        "b_v1": np.asarray(inputs["L0_attn_v1_b"], np.float32),
        "b_w2": np.asarray(inputs["L0_attn_w2_b"], np.float32),
        "b_f": np.asarray(inputs["L0_ffn_bias"], np.float32),
        "b_q": np.asarray(inputs["L1_attn_bq"], np.float32),
        "b_k": np.asarray(inputs["L1_attn_bk"], np.float32),
        "b_v": np.asarray(inputs["L1_attn_bv"], np.float32),
        "gamma": np.asarray(inputs["L1_attn_gamma"], np.float32),
        "beta": np.asarray(inputs["L1_attn_beta"], np.float32),
        "mu_r": np.asarray(inputs["L1_ffn_mu_r"], np.float32),
        "mu_k": np.asarray(inputs["L1_ffn_mu_k"], np.float32),
        "dssm": np.asarray(inputs["L0_attn_D_ssm"], np.float32),
        "gamma_row": np.asarray(inputs["L1_attn_gamma"], np.float32)[None, :],
        "beta_row": np.asarray(inputs["L1_attn_beta"], np.float32)[None, :],
        "cw": np.ascontiguousarray(
            np.asarray(inputs["L0_attn_conv_w"], np.float32)[:, 0, :]),
        "A_": np.asarray(inputs["L0_attn_A"], np.float32),
        "B_": np.asarray(inputs["L0_attn_B_ssm"], np.float32),
        "C_": np.asarray(inputs["L0_attn_C_ssm"], np.float32),
        "eye": np.eye(128, dtype=np.float32),
        "selw": rep(selw), "selm": rep(selm), "kvsel": rep(kvsel),
        "pvsel": rep(pvsel), "hmask": rep(hmask),
    }


def kernel(**inputs):
    x = np.asarray(inputs["L0_attn_x"])
    B, L, _ = x.shape
    ncore = 8
    nj = ncore // B
    TC = L // nj
    nc, _ = build(TC)
    in_maps = [_host_inputs(inputs, TC, (i // nj, i % nj)) for i in range(ncore)]
    res = run_bass_kernel_spmd(nc, in_maps, core_ids=list(range(ncore)))
    outp = np.empty((B, L, C), np.float32)
    for i in range(ncore):
        b, j = i // nj, i % nj
        outp[b, j * TC:(j + 1) * TC] = np.asarray(res.results[i]["out"])
    return outp


# version 25
# speedup vs baseline: 1.0766x; 1.0766x over previous
import os
import numpy as np
import concourse.bass as bass
import concourse.mybir as mybir
from concourse.bass_utils import run_bass_kernel_spmd
from contextlib import ExitStack

F32 = mybir.dt.float32
F32R = mybir.dt.float32r
BF16 = mybir.dt.bfloat16
I32 = mybir.dt.int32
U32 = mybir.dt.uint32
AF = mybir.ActivationFunctionType
ALU = mybir.AluOpType
AX = mybir.AxisListType

C = 1024
CCN = 8
S = 16
H = 16
EPS_LN = 1e-5
EPS_DEN = 1e-6


def build(TC, taps=(), upto=None):
    TCH = TC + 2
    NTB = TC // 512
    NTS = TC // 128
    NCK = TC // 256
    nc = bass.Bass("TRN2", target_bir_lowering=False)
    es = ExitStack()

    def di(name, shape, dt=F32):
        return nc.dram_tensor(name, shape, dt, kind="ExternalInput").ap()

    xfh_in = di("xfh", [CCN, 128, TCH])
    W = {n: di(n, [CCN, CCN, 128, 128], F32R) for n in
         ["w1t", "v1t", "w2t", "wqt", "wkt", "wvt", "wrt", "wkkt", "wvvt"]}
    wd_in = di("wd", [C, C])
    V = {n: di(n, [C]) for n in
         ["b_w1", "b_v1", "b_w2", "b_f", "b_q", "b_k", "b_v",
          "gamma", "beta", "mu_r", "mu_k", "dssm"]}
    grow_in = di("gamma_row", [1, C])
    brow_in = di("beta_row", [1, C])
    cw_in = di("cw", [C, 3])
    A_in = di("A_", [C, S])
    B_in = di("B_", [C, S])
    C_in = di("C_", [C, S])
    eye_in = di("eye", [128, 128])
    selw_in = di("selw", [128, 8])
    selm_in = di("selm", [128, 8])
    kvsel_in = di("kvsel", [128, 8])
    pvsel_in = di("pvsel", [128, 8])
    hmask_in = di("hmask", [128, 2])

    out = nc.dram_tensor("out", [TC, C], F32, kind="ExternalOutput").ap()
    tap_shapes = {"xn0": [CCN, 128, TC], "x1": [CCN, 128, TCH],
                  "xconv": [CCN, 128, TC], "sloc": [128, 128],
                  "hin": [128, 128], "yssm": [CCN, 128, TC],
                  "prod": [CCN, 128, TC], "x1r": [CCN, 128, TC],
                  "xn1": [CCN, 128, TC], "kvg": [128, 528],
                  "qf": [CCN, 128, TC], "att": [TC, C],
                  "xd": [CCN, 128, TC], "rwkv": [CCN, 128, TC],
                  "wdt": [CCN, CCN, 128, 128]}
    tap_aps = {t: nc.dram_tensor("tap_" + t, tap_shapes[t], F32,
                                 kind="ExternalOutput").ap() for t in taps}

    TCP = TCH + 6
    x1_h = nc.dram_tensor("x1_h", [CCN, 128, TCP], F32).ap()
    x1f_h = nc.dram_tensor("x1f_h", [CCN, 128, TC], F32).ap()
    at0_h = nc.dram_tensor("at0_h", [CCN, 128, TC], F32).ap()
    kf_h = nc.dram_tensor("kf_h", [CCN, 128, TC], F32).ap()
    vf_h = nc.dram_tensor("vf_h", [CCN, 128, TC], F32).ap()
    attn_h = nc.dram_tensor("attn_h", [TC, C], F32).ap()
    r_h = nc.dram_tensor("r_h", [CCN, 128, TC], F32).ap()
    wdt_h = nc.dram_tensor("wdt_h", [CCN, CCN, 128, 128], F32R).ap()
    cc1_in = nc.dram_tensor("cc1_in", [128, 128], F32).ap()
    cc1_out = nc.dram_tensor("cc1_out", [8 * 128, 128], F32).ap()
    cc2_in = nc.dram_tensor("cc2_in", [128, 528], F32).ap()
    cc2_out = nc.dram_tensor("cc2_out", [8 * 128, 528], F32).ap()

    with es:
        sbt = lambda n, shp, dt=F32: es.enter_context(nc.sbuf_tensor(n, shp, dt))
        SL1 = sbt("SL1", [128, CCN * TC])
        SL2 = sbt("SL2", [128, CCN * TC])
        WA = sbt("WA", [128, CCN * 128], F32R)
        WB = sbt("WB", [128, CCN * 128], F32R)
        TA = sbt("TA", [128, 2048])
        TB_ = sbt("TB_", [128, 2048])
        TCm = sbt("TCm", [128, max(TCH, 2048)])
        TI = sbt("TI", [128, 2048], I32)
        TU = sbt("TU", [128, 2048], U32)
        AR1 = sbt("AR1", [128, CCN * 256])
        AR2 = sbt("AR2", [128, CCN * 256])
        stat1 = sbt("stat1", [128, 512])
        stat2 = sbt("stat2", [128, 512])
        eye = sbt("eye_s", [128, 128])
        ones_ln = sbt("ones_ln", [128, 128])
        ones1 = sbt("ones1", [1, 128])
        onescol = sbt("onescol", [128, 1])
        epsc = sbt("epsc", [128, 1])
        zcol = sbt("zcol", [128, 1])
        vt = {n: sbt("t_" + n, [128, CCN]) for n in V}
        gbc = TI[:, 0:1024].bitcast(F32)
        bbc = TI[:, 1024:2048].bitcast(F32)
        cwt = sbt("cwt", [128, 3 * CCN])
        At = sbt("At", [128, S * CCN])
        abar = sbt("abar", [128, S * CCN])
        lnat = sbt("lnat", [128, S * CCN])
        cbt = sbt("cbt", [128, S * CCN])
        sloc = sbt("sloc", [128, 128])
        hin = sbt("hin", [128, 128])
        eye_b = sbt("eye_b", [128, 128], BF16)
        abK = sbt("abK", [128, S * CCN])
        dg = [sbt("dg0", [128, 128], BF16), sbt("dg1", [128, 128], BF16)]
        selwt = sbt("selwt", [128, 8])
        selmt = sbt("selmt", [128, 8])
        kvselt = sbt("kvselt", [128, 8])
        pvselt = sbt("pvselt", [128, 8])
        hmaskt = sbt("hmaskt", [128, 2])
        kvg = sbt("kvg", [128, 512])
        ksg = sbt("ksg", [128, 8])
        kslo = sbt("kslo", [128, 8])
        ksbd = sbt("ksbd", [128, CCN * 16])
        xprev = sbt("xprev", [128, CCN])
        xnh = sbt("xnh", [128, 2 * CCN])
        rden = sbt("rden", [128, 16 * NTS])
        ps = [es.enter_context(nc.psum_tensor("ps%d" % i, [128, 512], F32))
              for i in range(8)]
        kvbd = AR1[:, 0:1024]
        kvp = AR2[:, 0:512]
        pack = AR2[:, 512:1040]
        iota_f = AR1[:, 0:TC]
        grow = TA[0:1, 0:1024]
        brow = TA[0:1, 1024:2048]
        Bt = TB_[:, 0:128]
        Ct = TB_[:, 128:256]

        ops = []
        cnt = {}

        def bump(s, n=1):
            cnt[s] = cnt.get(s, 0) + n
            return cnt[s]

        def cur(s):
            return cnt.get(s, 0)

        def op(eng, emit, waits=(), incs=()):
            ops.append((eng, tuple(waits), emit, tuple(incs)))

        ENGS = ["pe", "dve", "act", "pool", "sp"]
        stage_marks = []

        def barrier():
            stage_marks.append(len(ops))
            dw = [(s, cur(s)) for s in cnt if s.startswith(("ld", "st", "wl"))
                  or s in ("out", "cc", "cc2")]
            op("sp", None, waits=dw)
            tgts = []
            for e in ENGS:
                b = "bar_" + e
                op(e, lambda g: g.drain(), incs=[(b, 1)])
                tgts.append((b, bump(b)))
            for e in ENGS:
                op(e, None, waits=tgts)

        def dma(src, dst, sem="ld", eng="sp", waits=()):
            if eng == "sp" and (sem.startswith("st") or sem == "out"):
                eng = "pool"
            op(eng, lambda g, s=src, d=dst: g.dma_start(out=d, in_=s),
               waits=waits, incs=[(sem, 16)])
            return bump(sem, 16)

        def allw():
            return [("pe", cur("pe")), ("dve", cur("dve")), ("act", cur("act")),
                    ("pool", cur("pool"))]

        def wslice(wname, dc, dst, waits=()):
            src = (W[wname] if wname in W else wdt_h)[dc].rearrange(
                "c p m -> p c m")
            sem = "wlA" if dst is WA else "wlB"
            dma(src, dst[:, :].rearrange("p (c m) -> p c m", m=128), sem=sem,
                waits=waits)
            return (sem, cur(sem))

        def sl1(cc):
            return SL1[:, cc * TC:(cc + 1) * TC]

        def sl2(cc):
            return SL2[:, cc * TC:(cc + 1) * TC]

        def proj_pipe(wname, mov_fn, post_fn, nblk, bw, movwaits=(),
                      banks=(2, 3, 4, 5), extra_mm=None, extra_waits=None):
            nb = len(banks)
            pe_after_dc = {}
            freed = []
            for dc in range(CCN):
                wbuf = WA if dc % 2 == 0 else WB
                wwaits = [("pe", pe_after_dc[dc - 2] if dc >= 2 else cur("pe"))]
                wl = wslice(wname, dc, wbuf, waits=wwaits)
                for blk in range(nblk):
                    g = dc * nblk + blk
                    waits = [wl]
                    if g == 0:
                        waits += list(movwaits)
                    if g >= nb:
                        waits.append(freed[g - nb])
                    bank = banks[g % nb]
                    if extra_waits is not None:
                        waits += extra_waits(dc, blk)

                    def e_mm(pe, w=wbuf, b=bank, blkx=blk, dcx=dc, w_=bw):
                        mm = None
                        for cc in range(CCN):
                            mm = pe.matmul(ps[b][:, :w_],
                                           w[:, cc * 128:(cc + 1) * 128],
                                           mov_fn(cc, blkx).bitcast(F32R),
                                           start=(cc == 0), stop=(cc == CCN - 1))
                        if extra_mm is not None:
                            mm = extra_mm(pe, dcx, blkx, b) or mm
                        return mm
                    op("pe", e_mm, waits=waits, incs=[("pe", 1)])
                    bump("pe")
                    freed.append(post_fn(dc, blk, bank, g))
                pe_after_dc[dc] = cur("pe")
            return cur("pe")

        def emit_quant(src, dst, s1, s2, s3, waits):
            def emit(v):
                G = C // 32
                v.drain()
                v.tensor_reduce(stat1[:, :G], src.rearrange("p (g k) -> p g k", k=32),
                                axis=AX.X, op=ALU.max, apply_absolute_value=True)
                v.drain()
                v.tensor_scalar(stat1[:, G:2 * G], stat1[:, :G], 1.0 / 7.5, 1e-30,
                                ALU.mult, ALU.max)
                v.drain()
                v.reciprocal(stat2[:, :G], stat1[:, G:2 * G])
                v.drain()
                rbc = stat2[:, :G].unsqueeze(2).broadcast_to([128, G, 32])
                v.tensor_tensor(s1.rearrange("p (g k) -> p g k", k=32),
                                src.rearrange("p (g k) -> p g k", k=32), rbc,
                                ALU.mult)
                v.drain()
                v.tensor_scalar(TU[:, :1024], s1.bitcast(U32), 23, 255,
                                ALU.logical_shift_right, ALU.bitwise_and)
                v.drain()
                v.tensor_scalar(s2, TU[:, :1024], 127.0, -1.0, ALU.max, ALU.mult)
                v.drain()
                v.tensor_scalar(TU[:, :1024], s2, 257.0, None, ALU.add)
                v.drain()
                v.tensor_scalar(TU[:, :1024], TU[:, :1024], 23, None,
                                ALU.logical_shift_left)
                v.drain()
                v.tensor_tensor(s1, s1, TU[:, :1024].bitcast(F32), ALU.mult)
                v.drain()
                v.tensor_copy(TI[:, :1024], s1)
                v.drain()
                v.tensor_copy(s3, TI[:, :1024])
                v.drain()
                v.tensor_tensor(dst, s1, s3, ALU.subtract)
                v.drain()
                v.tensor_scalar(s1, dst, 0.5, None, ALU.is_ge)
                v.drain()
                v.tensor_tensor(s3, s3, s1, ALU.add)
                v.drain()
                v.tensor_scalar(s1, dst, -0.5, None, ALU.is_le)
                v.drain()
                v.tensor_tensor(s1, s3, s1, ALU.subtract)
                v.drain()
                v.tensor_scalar(TU[:, :1024], s2, -1.0, -3.0, ALU.mult, ALU.add)
                v.drain()
                v.tensor_scalar(TU[:, :1024], TU[:, :1024], 23, None,
                                ALU.logical_shift_left)
                v.drain()
                v.tensor_tensor(s1, s1, TU[:, :1024].bitcast(F32), ALU.mult)
                v.drain()
                sbc = stat1[:, G:2 * G].unsqueeze(2).broadcast_to([128, G, 32])
                return v.tensor_tensor(dst.rearrange("p (g k) -> p g k", k=32),
                                       s1.rearrange("p (g k) -> p g k", k=32),
                                       sbc, ALU.mult)
            op("dve", emit, waits=waits, incs=[("dve", 1)])
            return bump("dve")

        dma(eye_in, eye[:])
        dma(selw_in, selwt[:])
        dma(selm_in, selmt[:])
        dma(kvsel_in, kvselt[:])
        dma(pvsel_in, pvselt[:])
        dma(hmask_in, hmaskt[:])
        dma(grow_in, grow)
        dma(brow_in, brow)
        for n in V:
            for cc in range(CCN):
                dma(V[n][cc * 128:(cc + 1) * 128].rearrange("(p o) -> p o", o=1),
                    vt[n][:, cc:cc + 1])
        for cc in range(CCN):
            dma(cw_in[cc * 128:(cc + 1) * 128, :], cwt[:, cc * 3:(cc + 1) * 3])
            dma(A_in[cc * 128:(cc + 1) * 128, :], At[:, cc * S:(cc + 1) * S])
            dma(B_in[cc * 128:(cc + 1) * 128, :], Bt[:, cc * S:(cc + 1) * S])
            dma(C_in[cc * 128:(cc + 1) * 128, :], Ct[:, cc * S:(cc + 1) * S])
        ld0 = cur("ld")

        def e_ms(g):
            g.memset(ones_ln[:], 1.0 / 1024.0)
            g.memset(ones1[:], 1.0)
            g.memset(onescol[:], 1.0)
            g.memset(epsc[:], EPS_LN)
            g.memset(hin[:], 0.0)
            g.memset(ksbd[:], 0.0)
            g.memset(zcol[:], 0.0)
            g.drain()
            return g.memset(xprev[:], 0.0)
        op("pool", e_ms, incs=[("pool", 1)])
        bump("pool")

        def e_ab(g):
            g.activation(abar[:], At[:], AF.Sigmoid)
            g.drain()
            return g.activation(lnat[:], abar[:], AF.Ln)
        op("act", e_ab, waits=[("ld", ld0)], incs=[("act", 1)])
        bump("act")
        op("dve", lambda g: g.tensor_tensor(cbt[:], Bt, Ct, ALU.mult),
           waits=[("ld", ld0)], incs=[("dve", 1)])
        bump("dve")
        op("dve", lambda g: (g.drain(), g.tensor_copy(eye_b[:], eye[:]))[1],
           waits=[("ld", ld0)], incs=[("dve", 1)])
        bump("dve")
        for cc in range(CCN):
            w0 = dma(wd_in[cc * 128:(cc + 1) * 128, :], TA[:, :1024],
                     waits=allw() + [("st", cur("st"))])
            q = emit_quant(TA[:, :1024], TB_[:, :1024], TCm[:, :1024],
                           TA[:, 1024:2048], TB_[:, 1024:2048],
                           waits=[("ld", w0)])
            for dcb in range(CCN):
                bnk, off = dcb % 2, (dcb // 2 % 2) * 128

                def e_tr(g, d=dcb, b=bnk, o=off):
                    return g.transpose(ps[b][:, o:o + 128],
                                       TB_[:, d * 128:(d + 1) * 128], eye[:])
                op("pe", e_tr, waits=[("dve", q), ("act", cur("act"))],
                   incs=[("pe", 1)])
                tp = bump("pe")

                def e_ev(a, d=dcb, b=bnk, o=off):
                    return a.activation(TCm[:, d * 128:(d + 1) * 128],
                                        ps[b][:, o:o + 128],
                                        AF.Copy, bias=0.0, scale=1.0)
                op("act", e_ev, waits=[("pe", tp), ("st", cur("st"))],
                   incs=[("act", 1)])
                ev = bump("act")
                dma(TCm[:, dcb * 128:(dcb + 1) * 128],
                    wdt_h[cc, dcb].bitcast(F32), sem="st", waits=[("act", ev)])
        if "wdt" in taps:
            dma(wdt_h[:, :, :, :], tap_aps["wdt"], sem="st",
                waits=[("st", cur("st"))])
        barrier()

        chunks = [(i * 256, 256) for i in range(NCK)] + \
                 [(NCK * 256, TCH - NCK * 256)]
        a1a_nrm = {}
        for ci, (c0, cwd) in enumerate(chunks):
            lw = allw() if ci == 0 else [("dve", a1a_nrm[ci - 1])]
            for cc in range(CCN):
                l0 = dma(xfh_in[cc, :, c0:c0 + cwd],
                         AR1[:, cc * 256:cc * 256 + cwd], sem="ldc", waits=lw)
                lw = ()

            def e_sq(a, w_=cwd):
                r = None
                for cc in range(CCN):
                    a.drain()
                    r = a.activation(AR2[:, cc * 256:cc * 256 + w_],
                                     AR1[:, cc * 256:cc * 256 + w_], AF.Square)
                return r
            op("act", e_sq, waits=[("ldc", cur("ldc")), ("dve", cur("dve"))],
               incs=[("act", 1)])
            a_sq = bump("act")

            def e_st(g, w_=cwd):
                for cc in range(CCN):
                    g.matmul(ps[4][:, :w_], ones_ln[:],
                             AR1[:, cc * 256:cc * 256 + w_],
                             start=(cc == 0), stop=(cc == CCN - 1))
                mm = None
                for cc in range(CCN):
                    mm = g.matmul(ps[5][:, :w_], ones_ln[:],
                                  AR2[:, cc * 256:cc * 256 + w_],
                                  start=(cc == 0), stop=(cc == CCN - 1))
                return mm
            op("pe", e_st, waits=[("act", a_sq)], incs=[("pe", 1)])
            p_st = bump("pe")

            def e_m2(a, w_=cwd):
                a.drain()
                return a.activation(stat1[:, :w_], ps[4][:, :w_], AF.Square)
            op("act", e_m2, waits=[("pe", p_st)], incs=[("act", 1)])
            a_m2 = bump("act")

            def e_var(v, w_=cwd):
                v.drain()
                return v.tensor_tensor(stat2[:, :w_], ps[5][:, :w_],
                                       stat1[:, :w_], ALU.subtract)
            op("dve", e_var, waits=[("act", a_m2)], incs=[("dve", 1)])
            d_v = bump("dve")
            op("act", lambda a, w_=cwd: (a.drain(), a.activation(
                stat1[:, :w_], stat2[:, :w_], AF.Sqrt, bias=epsc[:],
                scale=1.0))[1], waits=[("dve", d_v)], incs=[("act", 1)])
            a_sd = bump("act")
            op("dve", lambda v, w_=cwd: (v.drain(), v.reciprocal(
                stat2[:, :w_], stat1[:, :w_]))[1],
               waits=[("act", a_sd)], incs=[("dve", 1)])
            d_r = bump("dve")
            for cc in range(CCN):
                def e_n(v, ccx=cc, w_=cwd, c0x=c0):
                    v.drain()
                    v.tensor_tensor(AR2[:, ccx * 256:ccx * 256 + w_],
                                    AR1[:, ccx * 256:ccx * 256 + w_],
                                    ps[4][:, :w_], ALU.subtract)
                    v.drain()
                    v.tensor_tensor(AR2[:, ccx * 256:ccx * 256 + w_],
                                    AR2[:, ccx * 256:ccx * 256 + w_],
                                    stat2[:, :w_], ALU.mult)
                    v.drain()
                    lo, hi = c0x, c0x + w_
                    rlo, rhi = max(lo, 1), min(hi, TC + 1)
                    r = None
                    if rlo < rhi:
                        r = v.tensor_copy(sl1(ccx)[:, rlo - 1:rhi - 1]
                                          .bitcast(F32R),
                                          AR2[:, ccx * 256 + (rlo - lo):
                                              ccx * 256 + (rhi - lo)])
                    if lo == 0:
                        v.drain()
                        r = v.tensor_copy(xnh[:, 2 * ccx:2 * ccx + 1]
                                          .bitcast(F32R),
                                          AR2[:, ccx * 256:ccx * 256 + 1])
                    if hi == TCH:
                        v.drain()
                        r = v.tensor_copy(xnh[:, 2 * ccx + 1:2 * ccx + 2]
                                          .bitcast(F32R),
                                          AR2[:, ccx * 256 + (TCH - 1 - lo):
                                              ccx * 256 + (TCH - lo)])
                    return r
                op("dve", e_n, waits=[("dve", d_r)], incs=[("dve", 1)])
                bump("dve")
            a1a_nrm[ci] = cur("dve")
        a1a_dve = cur("dve")
        if "xn0" in taps:
            for cc in range(CCN):
                dma(sl1(cc), tap_aps["xn0"][cc], sem="st",
                    waits=[("dve", cur("dve"))])
        barrier()

        x1_st = {}
        dvh_cnt = {}

        def extra_x1(pe, dc, blk, bank):
            if blk != 0:
                return None
            mm = None
            wbuf = WA if dc % 2 == 0 else WB
            for cc in range(CCN):
                mm = pe.matmul(ps[6][:, 0:2], wbuf[:, cc * 128:(cc + 1) * 128],
                               xnh[:, 2 * cc:2 * cc + 2].bitcast(F32R),
                               start=(cc == 0), stop=(cc == CCN - 1))
            return mm

        def post_x1(dc, blk, bank, g):
            reg = g % 4
            pw = [("pe", cur("pe"))]
            if g >= 4:
                pw.append(("stx1r%d" % reg, cur("stx1r%d" % reg)))

            def e_ev(a, dcx=dc, b=bank, r=reg):
                a.drain()
                return a.activation(TA[:, r * 512:r * 512 + 512],
                                    ps[b][:, :512], AF.Identity,
                                    bias=vt["b_w1"][:, dcx:dcx + 1], scale=1.0)
            op("act", e_ev, waits=pw, incs=[("act", 1)])
            a_e = bump("act")
            dma(TA[:, reg * 512:reg * 512 + 512],
                x1_h[dc, :, 1 + blk * 512:1 + (blk + 1) * 512],
                sem="stx1r%d" % reg, waits=[("act", a_e)])
            if blk == 0:
                def e_evh(v, dcx=dc):
                    v.drain()
                    v.tensor_scalar(stat1[:, 440:442], ps[6][:, 0:2],
                                    vt["b_w1"][:, dcx:dcx + 1], None, ALU.add)
                    v.drain()
                    v.tensor_scalar(stat1[:, 440:441], stat1[:, 440:441],
                                    hmaskt[:, 0:1], None, ALU.mult)
                    v.drain()
                    return v.tensor_scalar(stat1[:, 441:442], stat1[:, 441:442],
                                           hmaskt[:, 1:2], None, ALU.mult)
                op("dve", e_evh, waits=[("pe", cur("pe")),
                                        ("stx1h", cur("stx1h"))],
                   incs=[("dve", 1)])
                d_h = bump("dve")
                dvh_cnt[dc] = d_h
                op("sp", lambda g, dcx=dc: g.dma_start(
                    out=x1_h[dcx, :, 0:1], in_=stat1[:, 440:441],
                    allow_slow_non_contiguous=True),
                   waits=[("dve", d_h)], incs=[("stx1h", 16)])
                bump("stx1h", 16)
                op("sp", lambda g, dcx=dc: g.dma_start(
                    out=x1_h[dcx, :, TC + 1:TC + 2], in_=stat1[:, 441:442],
                    allow_slow_non_contiguous=True),
                   incs=[("stx1h", 16)])
                bump("stx1h", 16)
            if blk == NTB - 1:
                x1_st[dc] = (tuple((("stx1r%d" % r), cur("stx1r%d" % r))
                             for r in range(4)), cur("stx1h"))
            return ("act", a_e)

        proj_pipe("w1t", lambda cc, blk: sl1(cc)[:, blk * 512:(blk + 1) * 512],
                  post_x1, NTB, 512, movwaits=[("dve", a1a_dve)],
                  extra_mm=extra_x1,
                  extra_waits=lambda dc, blk: (
                      [("dve", dvh_cnt[dc - 1])] if blk == 0 and dc >= 1
                      else []))
        a1b_act = cur("act")
        if "x1" in taps:
            for cc in range(CCN):
                dma(x1_h[cc, :, 0:TCH], tap_aps["x1"][cc], sem="st",
                    waits=[("stx1r%d" % r, cur("stx1r%d" % r))
                           for r in range(4)] +
                          [("stx1h", cur("stx1h"))])

        BQ = [TI[:, 0:TC].bitcast(F32), TU[:, 0:TC].bitcast(F32)]
        HB = [AR1[:, 0:TC // 2].bitcast(BF16), AR2[:, 0:TC // 2].bitcast(BF16)]
        DB = [TI[:, 0:TC // 2].bitcast(BF16), TU[:, 0:TC // 2].bitcast(BF16)]
        dvescan = {}
        ymm = {}
        yev = {}
        pend_x = []
        c1p = {}
        c23d = {}
        silua = {}
        for cc in range(CCN):
            lw = list(x1_st[cc][0]) + [("stx1h", x1_st[cc][1])]
            if cc == 0:
                lw += [("dve", cur("dve")), ("act", cur("act")),
                       ("pool", cur("pool"))]
            else:
                lw += [("pool", c1p[cc - 1]), ("dve", c23d[cc - 1])]
            l0 = dma(x1_h[cc, :, 0:TC], TB_[:, 0:TC], waits=lw)
            l0 = dma(x1_h[cc, :, TC:TCH], stat2[:, 448:450])

            def e_c1(g, ccx=cc):
                g.drain()
                return g.tensor_scalar(TA[:, :TC], TB_[:, 0:TC],
                                       cwt[:, ccx * 3:ccx * 3 + 1], 0.0,
                                       ALU.mult, ALU.add)
            c1w = [("ld", l0)]
            if cc > 0:
                c1w.append(("act", silua[cc - 1]))
            else:
                c1w += [("stx1r%d" % r, cur("stx1r%d" % r)) for r in range(4)]
            op("pool", e_c1, waits=c1w, incs=[("pool", 1)])
            pl = bump("pool")

            def e_c23(v, ccx=cc):
                v.drain()
                v.scalar_tensor_tensor(TCm[:, 0:TC - 1], TB_[:, 1:TC],
                                       cwt[:, ccx * 3 + 1:ccx * 3 + 2],
                                       TA[:, 0:TC - 1], ALU.mult, ALU.add)
                v.drain()
                v.scalar_tensor_tensor(TCm[:, TC - 1:TC],
                                       stat2[:, 448:449],
                                       cwt[:, ccx * 3 + 1:ccx * 3 + 2],
                                       TA[:, TC - 1:TC], ALU.mult, ALU.add)
                v.drain()
                v.scalar_tensor_tensor(TA[:, 0:TC - 2], TB_[:, 2:TC],
                                       cwt[:, ccx * 3 + 2:ccx * 3 + 3],
                                       TCm[:, 0:TC - 2], ALU.mult, ALU.add)
                v.drain()
                return v.scalar_tensor_tensor(TA[:, TC - 2:TC],
                                              stat2[:, 448:450],
                                              cwt[:, ccx * 3 + 2:ccx * 3 + 3],
                                              TCm[:, TC - 2:TC],
                                              ALU.mult, ALU.add)
            c1p[cc] = pl
            op("dve", e_c23, waits=[("pool", pl)], incs=[("dve", 1)])
            dv = bump("dve")
            c23d[cc] = dv
            op("act", lambda a, ccx=cc: (a.drain(), a.activation(
                sl2(ccx).bitcast(F32R), TA[:, :TC], AF.Silu))[1],
               waits=[("dve", dv)], incs=[("act", 1)])
            a_c = bump("act")
            silua[cc] = a_c
            for s in range(S + 1):
                g_ = cc * (S + 1) + s
                col = cc * S + min(s, S - 1)
                j = g_ % 2
                dstage = s < S

                def e_u(a, ccx=cc, colx=col, b=BQ[j], br=DB[j], d=dstage):
                    a.drain()
                    if d:
                        return a.activation(b, sl2(ccx), AF.Copy, bias=0.0,
                                            scale=cbt[:, colx:colx + 1])
                    return a.activation(br, sl2(ccx), AF.Copy, bias=0.0,
                                        scale=vt["dssm"][:, ccx:ccx + 1])
                uw = [("dve", dvescan[g_ - 2]), ("pe", ymm[g_ - 2])] \
                    if g_ >= 2 else [("act", a_c)]
                op("act", e_u, waits=uw, incs=[("act", 1)])
                a_u = bump("act")
                if dstage:
                    def e_s(v, b=BQ[j], hr=HB[j], colx=col):
                        return v.tensor_tensor_scan(
                            hr, abar[:, colx:colx + 1].broadcast_to([128, TC]),
                            b, zcol[:], ALU.mult, ALU.add)
                    sw = [("act", a_u)]
                    if g_ >= 2:
                        sw.append(("pe", ymm[g_ - 2]))
                    op("dve", e_s, waits=sw, incs=[("dve", 1)])
                    dvescan[g_] = bump("dve")

                    def e_x(a, h=HB[j], colx=col):
                        return a.activation(sloc[:, colx:colx + 1],
                                            h[:, TC - 1:TC], AF.Copy,
                                            bias=0.0, scale=1.0)
                    pend_x.append((e_x, dvescan[g_]))
                    if len(pend_x) > 1:
                        ex, exw = pend_x.pop(0)
                        op("act", ex, waits=[("dve", exw)], incs=[("act", 1)])
                        bump("act")
                else:
                    dvescan[g_] = cur("dve")

                def e_ym(pe, hr=(HB[j] if dstage else DB[j]), sx=s,
                         first=(s == 0), last=(s == S)):
                    mm = None
                    for blk in range(4):
                        mm = pe.matmul(ps[blk][:, :512], eye_b[:],
                                       hr[:, blk * 512:(blk + 1) * 512],
                                       start=first, stop=last)
                    return mm
                yw = [("dve", dvescan[g_])] if dstage else [("act", a_u)]
                if s == 0 and cc > 0:
                    yw.append(("act", yev[cc - 1]))
                if s == 0 and cc == 0:
                    yw.append(("act", a1b_act))
                op("pe", e_ym, waits=yw, incs=[("pe", 1)])
                ymm[g_] = bump("pe")
            while pend_x:
                ex, exw = pend_x.pop(0)
                op("act", ex, waits=[("dve", exw)], incs=[("act", 1)])
                bump("act")
            for blk in range(4):
                def e_ye(a, ccx=cc, b=blk):
                    a.drain()
                    return a.activation(
                        sl2(ccx)[:, b * 512:(b + 1) * 512].bitcast(F32R),
                        ps[b][:, :512], AF.Copy, bias=0.0, scale=1.0)
                op("act", e_ye, waits=[("pe", ymm[cc * (S + 1) + S])],
                   incs=[("act", 1)])
                yev[cc] = bump("act")
        if "xconv" in taps:
            for cc in range(CCN):
                dma(sl2(cc), tap_aps["xconv"][cc], sem="st",
                    waits=[("act", cur("act")), ("dve", cur("dve"))])
        if "sloc" in taps:
            dma(sloc[:], tap_aps["sloc"], sem="st", waits=[("dve", cur("dve"))])
        barrier()

        st1 = dma(sloc[:], cc1_in[:, :], sem="st")
        op("pool", lambda g: g.collective_compute(
            "AllGather", ALU.bypass, replica_groups=[list(range(8))],
            ins=[cc1_in.opt()], outs=[cc1_out.opt()]),
            waits=[("st", st1)], incs=[("cc", 1)])
        bump("cc")
        for i in range(8):
            li = dma(cc1_out[i * 128:(i + 1) * 128, :], TA[:, :128],
                     waits=allw() + [("cc", 1)])

            def e_co(g, ix=i):
                g.drain()
                return g.tensor_scalar(TB_[:, :128], lnat[:],
                                       selwt[:, ix:ix + 1], 0.0, ALU.mult,
                                       ALU.add)
            op("pool", e_co, waits=[("ld", li)], incs=[("pool", 1)])
            pl = bump("pool")
            op("act", lambda a: (a.drain(), a.activation(
                TCm[:, :128], TB_[:, :128], AF.Exp))[1],
               waits=[("pool", pl)], incs=[("act", 1)])
            ac = bump("act")

            def e_acc(v, ix=i):
                v.drain()
                v.tensor_scalar(TCm[:, 128:256], TCm[:, :128],
                                selmt[:, ix:ix + 1], None, ALU.mult)
                v.drain()
                v.tensor_tensor(TB_[:, :128], TA[:, :128], TCm[:, 128:256],
                                ALU.mult)
                v.drain()
                return v.tensor_tensor(hin[:], hin[:], TB_[:, :128], ALU.add)
            op("dve", e_acc, waits=[("act", ac)], incs=[("dve", 1)])
            bump("dve")
        if "hin" in taps:
            dma(hin[:], tap_aps["hin"], sem="st", waits=[("dve", cur("dve"))])
        barrier()

        op("pool", lambda g: (g.drain(), g.iota(
            AR1[:, 0:TC], pattern=[[1, TC]], base=1, channel_multiplier=0,
            allow_small_or_imprecise_dtypes=True))[1],
           waits=[("pe", cur("pe")), ("act", cur("act"))], incs=[("pool", 1)])
        iop = bump("pool")
        op("act", lambda a: (a.drain(), a.activation(
            abK[:], lnat[:], AF.Exp, bias=0.0, scale=float(TC // 2)))[1],
           incs=[("act", 1)])
        bump("act")
        corr_last = {}
        cmm = {}
        cevac = {}
        prod_dve = {}
        v1_ae = {}
        v1wl = {0: wslice("v1t", 0, WA, waits=[("pe", cur("pe"))])}
        for cc in range(CCN):
            for s in range(S):
                g2 = cc * S + s
                j = g2 % 2
                col = cc * S + s
                ew = [("pool", iop)]
                if g2 >= 2:
                    ew.append(("pe", cmm[g2 - 2]))

                def e_E(a, b=DB[j], colx=col):
                    a.drain()
                    return a.activation(b[:, 0:TC // 2], AR1[:, 0:TC // 2],
                                        AF.Exp, bias=0.0,
                                        scale=lnat[:, colx:colx + 1])
                op("act", e_E, waits=ew, incs=[("act", 1)])
                a_E = bump("act")

                def e_dg(v, b=dg[j], eb=DB[j], colx=col):
                    v.drain()
                    v.tensor_scalar(eb[:, TC // 2:TC], eb[:, 0:TC // 2],
                                    abK[:, colx:colx + 1], None, ALU.mult)
                    v.drain()
                    return v.tensor_scalar(b[:], eye[:],
                                           hin[:, colx:colx + 1],
                                           0.0, ALU.mult, ALU.add)
                dw = [("pe", cmm[g2 - 2])] if g2 >= 2 else []
                dw = dw + [("act", a_E)]
                op("dve", e_dg, waits=dw, incs=[("dve", 1)])
                d_g = bump("dve")

                def e_cm(pe, b=DB[j], d=dg[j], first=(s == 0),
                         last=(s == S - 1)):
                    mm = None
                    for blk in range(4):
                        mm = pe.matmul(ps[4 + blk][:, :512], d[:],
                                       b[:, blk * 512:(blk + 1) * 512],
                                       start=first, stop=last)
                    return mm
                cw = [("act", a_E), ("dve", d_g)]
                if s == 0 and cc >= 1:
                    cw.append(cevac[cc - 1])
                op("pe", e_cm, waits=cw, incs=[("pe", 1)])
                cmm[g2] = bump("pe")
            for blk in range(4):
                def e_ca(v, ccx=cc, blkx=blk, b=4 + blk):
                    v.drain()
                    return v.tensor_tensor(
                        sl2(ccx)[:, blkx * 512:(blkx + 1) * 512]
                        .bitcast(F32R),
                        sl2(ccx)[:, blkx * 512:(blkx + 1) * 512],
                        ps[b][:, :512], ALU.add)
                op("dve", e_ca, waits=[("pe", cmm[cc * S + S - 1])],
                   incs=[("dve", 1)])
                cevac[cc] = ("dve", bump("dve"))
            corr_last[cc] = cevac[cc]
            wbuf = WA if cc % 2 == 0 else WB
            if cc + 1 < CCN:
                v1wl[cc + 1] = wslice("v1t", cc + 1,
                                      WB if (cc + 1) % 2 == 1 else WA,
                                      waits=[("pe", cur("pe"))])
            for blk in range(4):
                g = cc * 4 + blk

                def e_mm(pe, w=wbuf, blkx=blk, b=blk):
                    mm = None
                    for k in range(CCN):
                        mm = pe.matmul(
                            ps[b][:, :512], w[:, k * 128:(k + 1) * 128],
                            sl1(k)[:, blkx * 512:(blkx + 1) * 512]
                            .bitcast(F32R),
                            start=(k == 0), stop=(k == CCN - 1))
                    return mm
                mw = [v1wl[cc]]
                if g >= 4:
                    mw.append(("act", v1_ae[g - 4]))
                op("pe", e_mm, waits=mw, incs=[("pe", 1)])
                pm = bump("pe")

                def e_ev(a, dcx=cc, b=blk, r=blk):
                    a.drain()
                    return a.activation(TB_[:, r * 512:r * 512 + 512],
                                        ps[b][:, :512], AF.Silu,
                                        bias=vt["b_v1"][:, dcx:dcx + 1],
                                        scale=1.0)
                ew2 = [("pe", pm)]
                if g >= 4:
                    ew2.append(("dve", prod_dve[g - 4]))
                op("act", e_ev, waits=ew2, incs=[("act", 1)])
                a_e = bump("act")
                v1_ae[g] = a_e

                def e_pr(v, dcx=cc, blkx=blk, r=blk):
                    v.drain()
                    return v.tensor_tensor(
                        sl2(dcx)[:, blkx * 512:(blkx + 1) * 512]
                        .bitcast(F32R),
                        sl2(dcx)[:, blkx * 512:(blkx + 1) * 512],
                        TB_[:, r * 512:r * 512 + 512], ALU.mult)
                op("dve", e_pr, waits=[("act", a_e), corr_last[cc]],
                   incs=[("dve", 1)])
                prod_dve[g] = bump("dve")
        if "prod" in taps:
            for cc in range(CCN):
                dma(sl2(cc), tap_aps["prod"][cc], sem="st",
                    waits=[("dve", cur("dve"))])
        barrier()

        w2wl = {0: wslice("w2t", 0, WA, waits=[("pe", cur("pe"))])}
        a_e3 = {}
        ng3 = CCN * NTB

        def b3a_group(g):
            dc, tb = divmod(g, NTB)
            wbuf = WA if dc % 2 == 0 else WB
            if tb == 0 and dc + 1 < CCN:
                w2wl[dc + 1] = wslice("w2t", dc + 1,
                                      WB if (dc + 1) % 2 == 1 else WA,
                                      waits=[("pe", cur("pe"))])
            waits = [w2wl[dc]]
            if g >= 2:
                waits.append(("act", a_e3[g - 2]))

            def e_mm(pe, w=wbuf, t=tb, b=6 + g % 2):
                mm = None
                for cc in range(CCN):
                    mm = pe.matmul(ps[b][:, :512], w[:, cc * 128:(cc + 1) * 128],
                                   sl2(cc)[:, t * 512:(t + 1) * 512].bitcast(F32R),
                                   start=(cc == 0), stop=(cc == CCN - 1))
                return mm
            op("pe", e_mm, waits=waits, incs=[("pe", 1)])
            pm = bump("pe")
            ew = [("pe", pm)]
            if g >= 2:
                ew.append(("st3r%d" % (g % 2), cur("st3r%d" % (g % 2))))

            def e_ev(a, dcx=dc, b=6 + g % 2, r=g % 2):
                a.drain()
                return a.activation(TCm[:, 1024 + r * 512:1024 + r * 512 + 512],
                                    ps[b][:, :512], AF.Identity,
                                    bias=vt["b_w2"][:, dcx:dcx + 1], scale=1.0)
            op("act", e_ev, waits=ew, incs=[("act", 1)])
            a_e3[g] = bump("act")
            dma(TCm[:, 1024 + (g % 2) * 512:1024 + (g % 2) * 512 + 512],
                at0_h[dc, :, tb * 512:(tb + 1) * 512], sem="st3r%d" % (g % 2),
                waits=[("act", a_e3[g])])

        g3 = 0
        for ts in range(NTS):
            for cc in range(CCN):
                bnk, off = cc % 2, (cc // 2 % 2) * 128

                def e_tf(g, ccx=cc, t=ts, b=bnk, o=off):
                    return g.transpose(ps[b][:, o:o + 128],
                                       sl1(ccx)[:, t * 128:(t + 1) * 128],
                                       eye[:])
                op("pe", e_tf, waits=[("dve", cur("dve")), ("act", cur("act"))],
                   incs=[("pe", 1)])
                tp = bump("pe")

                def e_ev(a, ccx=cc, b=bnk, o=off):
                    a.drain()
                    return a.activation(TA[:, ccx * 128:(ccx + 1) * 128],
                                        ps[b][:, o:o + 128], AF.Copy,
                                        bias=0.0, scale=1.0)
                op("act", e_ev, waits=[("pe", tp)], incs=[("act", 1)])
                bump("act")
            q = emit_quant(TA[:, :1024], TB_[:, :1024], TCm[:, :1024],
                           TA[:, 1024:2048], TB_[:, 1024:2048],
                           waits=[("act", cur("act"))])
            for cc in range(CCN):
                bnk, off = 4 + cc % 2, (cc // 2 % 2) * 128

                def e_tb(g, ccx=cc, b=bnk, o=off):
                    return g.transpose(ps[b][:, o:o + 128],
                                       TB_[:, ccx * 128:(ccx + 1) * 128],
                                       eye[:])
                op("pe", e_tb, waits=[("dve", q), ("act", cur("act"))],
                   incs=[("pe", 1)])
                tp = bump("pe")

                def e_ev2(a, ccx=cc, t=ts, b=bnk, o=off):
                    a.drain()
                    return a.activation(sl1(ccx)[:, t * 128:(t + 1) * 128]
                                        .bitcast(F32R),
                                        ps[b][:, o:o + 128], AF.Copy,
                                        bias=0.0, scale=1.0)
                op("act", e_ev2, waits=[("pe", tp)], incs=[("act", 1)])
                bump("act")
            while g3 < ng3 * (ts + 1) // NTS:
                b3a_group(g3)
                g3 += 1
        while g3 < ng3:
            b3a_group(g3)
            g3 += 1
        if "xd" in taps:
            for cc in range(CCN):
                dma(sl1(cc), tap_aps["xd"][cc], sem="st",
                    waits=[("act", cur("act"))])
        barrier()

        f0wl = {0: wslice("wdt", 0, WA, waits=[("pe", cur("pe"))])}
        tr_pe = {}
        sum_dve = {}
        b3b_ae = {}
        for g in range(ng3):
            dc, tb = divmod(g, NTB)
            wbuf = WA if dc % 2 == 0 else WB
            if tb == 0 and dc + 1 < CCN:
                f0wl[dc + 1] = wslice("wdt", dc + 1,
                                      WB if (dc + 1) % 2 == 1 else WA,
                                      waits=[("pe", cur("pe"))])
            xl = [("dve", sum_dve[g - 2])] if g >= 2 else [("dve", cur("dve"))]
            xl4 = dma(xfh_in[dc, :, 1 + tb * 512:1 + (tb + 1) * 512],
                      TCm[:, (g % 2) * 512:(g % 2) * 512 + 512],
                      sem="ldx", waits=xl)
            aw = [("dve", sum_dve[g - 4])] if g >= 4 else \
                [("st3r0", cur("st3r0")), ("st3r1", cur("st3r1"))]
            al = dma(at0_h[dc, :, tb * 512:(tb + 1) * 512],
                     TB_[:, (g % 4) * 512:(g % 4) * 512 + 512], sem="lda",
                     waits=aw)
            mmw = [f0wl[dc], ("ldx", cur("ldx"))]
            if g >= 4:
                mmw.append(("act", b3b_ae[g - 4]))

            def e_mm(pe, w=wbuf, t=tb, b=2 + g % 4):
                mm = None
                for cc in range(CCN):
                    mm = pe.matmul(ps[b][:, :512],
                                   w[:, cc * 128:(cc + 1) * 128],
                                   sl1(cc)[:, t * 512:(t + 1) * 512]
                                   .bitcast(F32R),
                                   start=(cc == 0), stop=(cc == CCN - 1))
                return mm
            op("pe", e_mm, waits=mmw, incs=[("pe", 1)])
            pm = bump("pe")
            tr_pe[g] = pm
            ew = [("pe", pm)]
            if g >= 4:
                ew.append(("st4r%d" % (g % 4), cur("st4r%d" % (g % 4))))

            def e_ev(a, dcx=dc, b=2 + g % 4, r=g % 4):
                a.drain()
                return a.activation(TA[:, r * 512:r * 512 + 512],
                                    ps[b][:, :512], AF.Identity,
                                    bias=vt["b_f"][:, dcx:dcx + 1], scale=1.0)
            op("act", e_ev, waits=ew, incs=[("act", 1)])
            b3b_ae[g] = bump("act")
            a_e = ("act", b3b_ae[g])

            def e_sum(v, r=g % 4, b=g % 2):
                v.drain()
                v.tensor_tensor(TA[:, r * 512:r * 512 + 512],
                                TA[:, r * 512:r * 512 + 512],
                                TB_[:, r * 512:r * 512 + 512], ALU.add)
                v.drain()
                return v.tensor_tensor(TA[:, r * 512:r * 512 + 512],
                                       TA[:, r * 512:r * 512 + 512],
                                       TCm[:, b * 512:b * 512 + 512], ALU.add)
            op("dve", e_sum, waits=[a_e, ("lda", al), ("ldx", cur("ldx"))],
               incs=[("dve", 1)])
            sum_dve[g] = bump("dve")
            dma(TA[:, (g % 4) * 512:(g % 4) * 512 + 512],
                x1f_h[dc, :, tb * 512:(tb + 1) * 512], sem="st4r%d" % (g % 4),
                waits=[("dve", sum_dve[g])])
        if "x1r" in taps:
            for cc in range(CCN):
                dma(x1f_h[cc], tap_aps["x1r"][cc], sem="st",
                    waits=[("st4r%d" % r, cur("st4r%d" % r))
                           for r in range(4)])
        barrier()

        for ck in range(NCK):
            c0 = ck * 256
            lds = allw() + [("st4r%d" % r, cur("st4r%d" % r))
                            for r in range(4)]
            for cc in range(CCN):
                dma(x1f_h[cc, :, c0:c0 + 256], AR1[:, cc * 256:(cc + 1) * 256],
                    sem="ld", waits=lds)
                lds = ()
            lA = cur("ld")

            def e_sq(a):
                r = None
                for cc in range(CCN):
                    a.drain()
                    r = a.activation(AR2[:, cc * 256:(cc + 1) * 256],
                                     AR1[:, cc * 256:(cc + 1) * 256], AF.Square)
                return r
            op("act", e_sq, waits=[("ld", lA)], incs=[("act", 1)])
            a_sq = bump("act")

            def e_st(g):
                for cc in range(CCN):
                    g.matmul(ps[4][:, :256], ones_ln[:],
                             AR1[:, cc * 256:(cc + 1) * 256],
                             start=(cc == 0), stop=(cc == CCN - 1))
                mm = None
                for cc in range(CCN):
                    mm = g.matmul(ps[5][:, :256], ones_ln[:],
                                  AR2[:, cc * 256:(cc + 1) * 256],
                                  start=(cc == 0), stop=(cc == CCN - 1))
                return mm
            op("pe", e_st, waits=[("act", a_sq), ("ld", lA)], incs=[("pe", 1)])
            p_st = bump("pe")

            def e_m2(a):
                a.drain()
                return a.activation(stat1[:, :256], ps[4][:, :256], AF.Square)
            op("act", e_m2, waits=[("pe", p_st)], incs=[("act", 1)])
            a_m2 = bump("act")

            def e_var(v):
                v.drain()
                return v.tensor_tensor(stat2[:, :256], ps[5][:, :256],
                                       stat1[:, :256], ALU.subtract)
            op("dve", e_var, waits=[("act", a_m2)], incs=[("dve", 1)])
            d_v = bump("dve")
            op("act", lambda a: (a.drain(), a.activation(
                stat1[:, :256], stat2[:, :256], AF.Sqrt, bias=epsc[:],
                scale=1.0))[1], waits=[("dve", d_v)], incs=[("act", 1)])
            a_sd = bump("act")
            op("dve", lambda v: (v.drain(), v.reciprocal(
                stat2[:, :256], stat1[:, :256]))[1],
               waits=[("act", a_sd)], incs=[("dve", 1)])
            d_r = bump("dve")
            for cc in range(CCN):
                def e_n(v, ccx=cc, c0x=c0):
                    v.drain()
                    v.tensor_tensor(AR2[:, ccx * 256:(ccx + 1) * 256],
                                    AR1[:, ccx * 256:(ccx + 1) * 256],
                                    ps[4][:, :256], ALU.subtract)
                    v.drain()
                    return v.tensor_tensor(sl1(ccx)[:, c0x:c0x + 256]
                                           .bitcast(F32R),
                                           AR2[:, ccx * 256:(ccx + 1) * 256],
                                           stat2[:, :256], ALU.mult)
                op("dve", e_n, waits=[("dve", d_r)], incs=[("dve", 1)])
                bump("dve")
        if "xn1" in taps:
            for cc in range(CCN):
                dma(sl1(cc), tap_aps["xn1"][cc], sem="st",
                    waits=[("dve", cur("dve"))])
        barrier()

        for (wn, bn, fn, hap, stn) in [("wkt", "b_k", AF.Relu, kf_h, "stk"),
                                       ("wvt", "b_v", AF.Identity, vf_h, "stv")]:
            def post_kv(dc, blk, bank, g, fnx=fn, bnx=bn, hapx=hap, stx=stn):
                reg = g % 4
                sreg = stx + "r%d" % reg
                pw = [("pe", cur("pe"))]
                pw.append((sreg, cur(sreg)))
                if stx == "stv" and g < 4:
                    pw.append(("stkr%d" % reg, cur("stkr%d" % reg)))

                def e_ev(a, dcx=dc, b=bank, r=reg):
                    a.drain()
                    return a.activation(TA[:, r * 512:r * 512 + 512],
                                        ps[b][:, :512], fnx,
                                        bias=vt[bnx][:, dcx:dcx + 1], scale=1.0)
                op("act", e_ev, waits=pw, incs=[("act", 1)])
                a_e = bump("act")
                dma(TA[:, reg * 512:reg * 512 + 512],
                    hapx[dc, :, blk * 512:(blk + 1) * 512], sem=sreg,
                    waits=[("act", a_e)])
                return ("act", a_e)
            proj_pipe(wn, lambda cc, blk: sl1(cc)[:, blk * 512:(blk + 1) * 512],
                      post_kv, NTB, 512)
        barrier()

        op("pool", lambda g: (g.memset(kvp, 0.0), g.memset(kslo[:], 0.0),
                              g.drain())[2], waits=allw(), incs=[("pool", 1)])
        bump("pool")
        for ts in range(NTS):
            lk = allw() + [("stkr%d" % r, cur("stkr%d" % r))
                           for r in range(4)] + \
                 [("stvr%d" % r, cur("stvr%d" % r)) for r in range(4)] + \
                 [("pool", cur("pool"))]
            for cc in range(CCN):
                dma(kf_h[cc, :, ts * 128:(ts + 1) * 128],
                    TCm[:, cc * 128:(cc + 1) * 128], sem="ld", waits=lk)
                lk = ()
                dma(vf_h[cc, :, ts * 128:(ts + 1) * 128],
                    TCm[:, 1024 + cc * 128:1024 + (cc + 1) * 128], sem="ld")
            lA = cur("ld")
            for j, dst in [(0, TA), (1, TB_)]:
                for cc in range(CCN):
                    bnk, off = j, (cc % 4) * 128

                    def e_tr(g, ccx=cc, jx=j, b=bnk, o=off):
                        return g.transpose(ps[b][:, o:o + 128],
                                           TCm[:, jx * 1024 + ccx * 128:
                                               jx * 1024 + (ccx + 1) * 128],
                                           eye[:])
                    op("pe", e_tr,
                       waits=[("ld", lA), ("act", cur("act"))] if cc == 0
                       else [("act", cur("act"))], incs=[("pe", 1)])
                    tp = bump("pe")

                    def e_ev(a, ccx=cc, d=dst, b=bnk, o=off):
                        a.drain()
                        return a.activation(d[:, ccx * 128:(ccx + 1) * 128],
                                            ps[b][:, o:o + 128], AF.Copy,
                                            bias=0.0, scale=1.0)
                    op("act", e_ev, waits=[("pe", tp)], incs=[("act", 1)])
                    bump("act")
            a_tm = cur("act")

            def e_kv(g, t=ts):
                mm = None
                for h in range(H):
                    r0 = 64 * (h % 2)
                    mm = g.matmul(ps[6][r0:r0 + 64, (h // 2) * 64:
                                        (h // 2) * 64 + 64],
                                  TA[:, h * 64:(h + 1) * 64],
                                  TB_[:, h * 64:(h + 1) * 64],
                                  start=True, stop=True, skip_group_check=True)
                for h in range(H):
                    r0 = 64 * (h % 2)
                    mm = g.matmul(ps[7][r0:r0 + 64, (h // 2):(h // 2) + 1],
                                  TA[:, h * 64:(h + 1) * 64], onescol[:],
                                  start=True, stop=True, skip_group_check=True)
                return mm
            op("pe", e_kv, waits=[("act", a_tm), ("dve", cur("dve"))],
               incs=[("pe", 1)])
            pkv = bump("pe")

            def e_kacc(v):
                v.drain()
                v.tensor_tensor(kvp, kvp, ps[6][:, :512], ALU.add)
                v.drain()
                return v.tensor_tensor(kslo[:], kslo[:], ps[7][:, :8], ALU.add)
            op("dve", e_kacc, waits=[("pe", pkv)], incs=[("dve", 1)])
            bump("dve")
        def e_evkv(v):
            v.drain()
            v.tensor_copy(pack[:, 0:512], kvp)
            v.drain()
            v.tensor_copy(pack[:, 512:520], kslo[:])
            v.drain()
            r = None
            for cc in range(CCN):
                r = v.tensor_copy(pack[:, 520 + cc:521 + cc],
                                  sl1(cc)[:, TC - 1:TC])
                v.drain()
            return r
        op("dve", e_evkv, waits=[("pe", cur("pe"))], incs=[("dve", 1)])
        dpk = bump("dve")
        st2 = dma(pack, cc2_in[:, :], sem="st", waits=[("dve", dpk)])
        op("pool", lambda g: g.collective_compute(
            "AllGather", ALU.bypass, replica_groups=[list(range(8))],
            ins=[cc2_in.opt()], outs=[cc2_out.opt()]),
            waits=[("st", st2)], incs=[("cc2", 1)])
        bump("cc2")
        op("pool", lambda g: (g.memset(kvg[:], 0.0), g.memset(ksg[:], 0.0),
                              g.drain(), g.memset(xprev[:], 0.0))[3],
           waits=[("dve", cur("dve"))], incs=[("pool", 1)])
        bump("pool")
        for i in range(8):
            li = dma(cc2_out[i * 128:(i + 1) * 128, :], TA[:, :528],
                     waits=allw() + [("cc2", 1)])

            def e_red(v, ix=i):
                v.drain()
                v.scalar_tensor_tensor(kvg[:], TA[:, 0:512],
                                       kvselt[:, ix:ix + 1], kvg[:],
                                       ALU.mult, ALU.add)
                v.drain()
                v.scalar_tensor_tensor(ksg[:], TA[:, 512:520],
                                       kvselt[:, ix:ix + 1], ksg[:],
                                       ALU.mult, ALU.add)
                v.drain()
                return v.scalar_tensor_tensor(xprev[:], TA[:, 520:528],
                                              pvselt[:, ix:ix + 1], xprev[:],
                                              ALU.mult, ALU.add)
            op("dve", e_red, waits=[("ld", li)], incs=[("dve", 1)])
            bump("dve")

        def e_bd(v):
            r = None
            for dcc in range(CCN):
                v.drain()
                v.tensor_copy(ksbd[0:64, dcc * 16 + 2 * dcc:
                                   dcc * 16 + 2 * dcc + 1],
                              ksg[0:64, dcc:dcc + 1])
                v.drain()
                r = v.tensor_copy(ksbd[64:128, dcc * 16 + 2 * dcc + 1:
                                       dcc * 16 + 2 * dcc + 2],
                                  ksg[64:128, dcc:dcc + 1])
            return r
        op("dve", e_bd, waits=[("dve", cur("dve")), ("pool", cur("pool"))],
           incs=[("dve", 1)])
        bump("dve")
        op("pool", lambda g: (g.memset(kvbd, 0.0), g.drain())[1],
           waits=[("dve", cur("dve"))], incs=[("pool", 1)])
        bump("pool")

        def e_kvbd(v):
            r = None
            for j in range(CCN):
                v.drain()
                v.tensor_copy(kvbd[0:64, j * 128:j * 128 + 64],
                              kvg[0:64, j * 64:(j + 1) * 64])
                v.drain()
                r = v.tensor_copy(kvbd[64:128, j * 128 + 64:j * 128 + 128],
                                  kvg[64:128, j * 64:(j + 1) * 64])
            return r
        op("dve", e_kvbd, waits=[("pool", cur("pool"))], incs=[("dve", 1)])
        bump("dve")
        if "kvg" in taps:
            def e_tapkv(v):
                v.drain()
                v.tensor_copy(pack[:, 0:512], kvg[:])
                v.drain()
                v.tensor_copy(pack[:, 512:520], ksg[:])
                v.drain()
                return v.tensor_copy(pack[:, 520:528], xprev[:])
            op("dve", e_tapkv, waits=[("dve", cur("dve"))], incs=[("dve", 1)])
            bump("dve")
            dma(pack, tap_aps["kvg"], sem="st", waits=[("dve", cur("dve"))])
        barrier()

        def post_q(dc, blk, bank, g):
            def e_ev(a, dcx=dc, b=bank, blkx=blk):
                a.drain()
                return a.activation(sl2(dcx)[:, blkx * 512:(blkx + 1) * 512]
                                    .bitcast(F32R),
                                    ps[b][:, :512], AF.Relu,
                                    bias=vt["b_q"][:, dcx:dcx + 1], scale=1.0)
            op("act", e_ev, waits=[("pe", cur("pe"))], incs=[("act", 1)])
            return ("act", bump("act"))
        proj_pipe("wqt", lambda cc, blk: sl1(cc)[:, blk * 512:(blk + 1) * 512],
                  post_q, NTB, 512)
        if "qf" in taps:
            for cc in range(CCN):
                dma(sl2(cc), tap_aps["qf"][cc], sem="st",
                    waits=[("act", cur("act"))])
        barrier()
        gld = dma(grow_in, grow, waits=allw())
        gld = dma(brow_in, brow)
        for half in range(2):
            def e_g(g, h=half):
                g.matmul(ps[0][:, :512], ones1[:],
                         grow[:, h * 512:(h + 1) * 512], start=True, stop=True)
                return g.matmul(ps[1][:, :512], ones1[:],
                                brow[:, h * 512:(h + 1) * 512],
                                start=True, stop=True)
            op("pe", e_g, waits=[("pe", cur("pe")), ("dve", cur("dve")),
                                 ("ld", gld)],
               incs=[("pe", 1)])
            pb = bump("pe")

            def e_gev(v, h=half):
                v.drain()
                v.tensor_copy(gbc[:, h * 512:(h + 1) * 512], ps[0][:, :512])
                v.drain()
                return v.tensor_copy(bbc[:, h * 512:(h + 1) * 512],
                                     ps[1][:, :512])
            op("dve", e_gev, waits=[("pe", pb)], incs=[("dve", 1)])
            bump("dve")
        for ts in range(NTS):
            def e_dn(g, t=ts):
                mm = None
                for cc in range(CCN):
                    mm = g.matmul(ps[2][:, :16],
                                  sl2(cc)[:, t * 128:(t + 1) * 128],
                                  ksbd[:, cc * 16:(cc + 1) * 16],
                                  start=(cc == 0), stop=(cc == CCN - 1))
                return mm
            op("pe", e_dn, waits=allw() if ts == 0 else [("dve", cur("dve"))],
               incs=[("pe", 1)])
            pd = bump("pe")

            def e_rd(v, t=ts):
                v.drain()
                v.tensor_scalar(stat1[:, :16], ps[2][:, :16], EPS_DEN, None,
                                ALU.max)
                v.drain()
                return v.reciprocal(rden[:, t * 16:(t + 1) * 16],
                                    stat1[:, :16])
            op("dve", e_rd, waits=[("pe", pd)], incs=[("dve", 1)])
            drd = bump("dve")

            def e_cx(g, t=ts):
                mm = None
                for j in range(CCN):
                    bnk = 3 + j // 4
                    mm = g.matmul(ps[bnk][:, (j % 4) * 128:(j % 4) * 128 + 128],
                                  sl2(j)[:, t * 128:(t + 1) * 128],
                                  kvbd[:, j * 128:(j + 1) * 128],
                                  start=True, stop=True, skip_group_check=True)
                return mm
            op("pe", e_cx, waits=[("dve", drd)], incs=[("pe", 1)])
            pc = bump("pe")

            def e_tr(g, t=ts):
                g.drain()
                mm = None
                for cc in range(CCN):
                    b, o = cc // 4, (cc % 4) * 128
                    mm = g.transpose(ps[b][:, o:o + 128],
                                     sl1(cc)[:, t * 128:(t + 1) * 128], eye[:])
                return mm
            op("pe", e_tr, incs=[("pe", 1)])
            ptr = bump("pe")

            def e_ep(v, t=ts):
                v.drain()
                for h in range(H):
                    bnk = 3 + h // 8
                    v.tensor_scalar(TA[:, h * 64:(h + 1) * 64],
                                    ps[bnk][:, (h % 8) * 64:(h % 8) * 64 + 64],
                                    rden[:, t * 16 + h:t * 16 + h + 1], None,
                                    ALU.mult)
                v.drain()
                v.tensor_tensor(TA[:, 0:512], TA[:, 0:512], ps[0][:, :512],
                                ALU.add)
                v.drain()
                v.tensor_tensor(TA[:, 512:1024], TA[:, 512:1024],
                                ps[1][:, :512], ALU.add)
                v.drain()
                v.tensor_reduce(stat1[:, 0:1], TA[:, :1024], axis=AX.X,
                                op=ALU.add)
                v.drain()
                return v.tensor_scalar(stat1[:, 1:2], stat1[:, 0:1],
                                       1.0 / 1024.0, None, ALU.mult)
            op("dve", e_ep, waits=[("pe", pc), ("pe", ptr), ("act", cur("act")),
                                   ("sta", cur("sta"))],
               incs=[("dve", 1)])
            dep = bump("dve")
            op("act", lambda a: (a.drain(), a.activation(
                TB_[:, :1024], TA[:, :1024], AF.Square))[1],
               waits=[("dve", dep)], incs=[("act", 1)])
            asq = bump("act")

            def e_st3(v):
                v.drain()
                v.tensor_reduce(stat1[:, 2:3], TB_[:, :1024], axis=AX.X,
                                op=ALU.add)
                v.drain()
                v.tensor_scalar(stat1[:, 3:4], stat1[:, 2:3], 1.0 / 1024.0,
                                None, ALU.mult)
                v.drain()
                v.tensor_tensor(stat1[:, 4:5], stat1[:, 1:2], stat1[:, 1:2],
                                ALU.mult)
                v.drain()
                return v.tensor_tensor(stat1[:, 5:6], stat1[:, 3:4],
                                       stat1[:, 4:5], ALU.subtract)
            op("dve", e_st3, waits=[("act", asq)], incs=[("dve", 1)])
            dst3 = bump("dve")
            op("act", lambda a: (a.drain(), a.activation(
                stat1[:, 6:7], stat1[:, 5:6], AF.Sqrt, bias=epsc[:],
                scale=1.0))[1], waits=[("dve", dst3)], incs=[("act", 1)])
            asd = bump("act")

            def e_fin(v, t=ts):
                v.drain()
                v.reciprocal(stat1[:, 7:8], stat1[:, 6:7])
                v.drain()
                v.tensor_scalar(TA[:, :1024], TA[:, :1024], stat1[:, 1:2],
                                stat1[:, 7:8], ALU.subtract, ALU.mult)
                v.drain()
                v.tensor_tensor(TA[:, :1024], TA[:, :1024], gbc[:], ALU.mult)
                v.drain()
                return v.tensor_tensor(TA[:, :1024], TA[:, :1024], bbc[:],
                                       ALU.add)
            op("dve", e_fin, waits=[("act", asd)], incs=[("dve", 1)])
            dfin = bump("dve")
            dma(TA[:, :1024], attn_h[ts * 128:(ts + 1) * 128, :], sem="sta",
                waits=[("dve", dfin)])
        if "att" in taps:
            dma(attn_h[:, :], tap_aps["att"], sem="st",
                waits=[("sta", cur("sta"))])
        barrier()

        for cc in range(CCN):
            def e_mix(v, ccx=cc):
                v.drain()
                v.tensor_tensor(TB_[:, 1:TC], sl1(ccx)[:, 1:TC],
                                sl1(ccx)[:, 0:TC - 1], ALU.subtract)
                v.drain()
                v.tensor_tensor(TB_[:, 0:1], sl1(ccx)[:, 0:1],
                                xprev[:, ccx:ccx + 1], ALU.subtract)
                v.drain()
                v.scalar_tensor_tensor(sl2(ccx)[:, 1:TC].bitcast(F32R),
                                       TB_[:, 1:TC],
                                       vt["mu_k"][:, ccx:ccx + 1],
                                       sl1(ccx)[:, 0:TC - 1], ALU.mult, ALU.add)
                v.drain()
                v.scalar_tensor_tensor(sl2(ccx)[:, 0:1].bitcast(F32R),
                                       TB_[:, 0:1],
                                       vt["mu_k"][:, ccx:ccx + 1],
                                       xprev[:, ccx:ccx + 1], ALU.mult, ALU.add)
                v.drain()
                v.scalar_tensor_tensor(TA[:, 1:TC], TB_[:, 1:TC],
                                       vt["mu_r"][:, ccx:ccx + 1],
                                       sl1(ccx)[:, 0:TC - 1], ALU.mult, ALU.add)
                v.drain()
                v.scalar_tensor_tensor(TA[:, 0:1], TB_[:, 0:1],
                                       vt["mu_r"][:, ccx:ccx + 1],
                                       xprev[:, ccx:ccx + 1], ALU.mult, ALU.add)
                v.drain()
                return v.tensor_copy(sl1(ccx).bitcast(F32R), TA[:, :TC])
            op("dve", e_mix, waits=allw() if cc == 0 else (), incs=[("dve", 1)])
            bump("dve")
        dmx = cur("dve")

        def post_r(dc, blk, bank, g):
            reg = g % 4
            sreg = "strr%d" % reg
            pw = [("pe", cur("pe"))]
            if g >= 4:
                pw.append((sreg, cur(sreg)))

            def e_ev(a, b=bank, r=reg):
                a.drain()
                return a.activation(TA[:, r * 512:r * 512 + 512],
                                    ps[b][:, :512], AF.Sigmoid)
            op("act", e_ev, waits=pw, incs=[("act", 1)])
            a_e = bump("act")
            dma(TA[:, reg * 512:reg * 512 + 512],
                r_h[dc, :, blk * 512:(blk + 1) * 512], sem=sreg,
                waits=[("act", a_e)])
            return ("act", a_e)
        proj_pipe("wrt", lambda cc, blk: sl1(cc)[:, blk * 512:(blk + 1) * 512],
                  post_r, NTB, 512, movwaits=[("dve", dmx)])
        barrier()

        ka_dve = {}

        ka_dve2 = {}

        def post_ka(dc, blk, bank, g):
            reg = g % 4
            pw = [("pe", cur("pe"))]
            if g >= 4:
                pw.append(("dve", ka_dve2[g - 4]))

            def e_ev(a, b=bank, r=reg):
                a.drain()
                return a.activation(TB_[:, r * 512:r * 512 + 512],
                                    ps[b][:, :512], AF.Relu)
            op("act", e_ev, waits=pw, incs=[("act", 1)])
            a_e = bump("act")

            def e_sqk(v, dcx=dc, blkx=blk, r=reg):
                v.drain()
                return v.tensor_tensor(
                    sl1(dcx)[:, blkx * 512:(blkx + 1) * 512].bitcast(F32R),
                    TB_[:, r * 512:r * 512 + 512],
                    TB_[:, r * 512:r * 512 + 512], ALU.mult)
            op("dve", e_sqk, waits=[("act", a_e)], incs=[("dve", 1)])
            ka_dve2[g] = bump("dve")
            return ("act", a_e)
        proj_pipe("wkkt", lambda cc, blk: sl2(cc)[:, blk * 512:(blk + 1) * 512],
                  post_ka, NTB, 512)
        barrier()

        o_dve = {}

        def post_o(dc, blk, bank, g):
            reg = g % 4
            lw = [("dve", cur("dve"))] if g < 4 else [("dve", o_dve[g - 4])]
            if g == 0:
                lw += [("strr%d" % r, cur("strr%d" % r)) for r in range(4)]
            lr = dma(r_h[dc, :, blk * 512:(blk + 1) * 512],
                     TB_[:, reg * 512:reg * 512 + 512], sem="ldr%d" % reg,
                     waits=lw)

            def e_o(v, dcx=dc, blkx=blk, b=bank, r=reg):
                v.drain()
                return v.tensor_tensor(
                    sl2(dcx)[:, blkx * 512:(blkx + 1) * 512].bitcast(F32R),
                    TB_[:, r * 512:r * 512 + 512], ps[b][:, :512], ALU.mult)
            op("dve", e_o, waits=[("pe", cur("pe")), ("ldr%d" % reg, lr)],
               incs=[("dve", 1)])
            o_dve[g] = bump("dve")
            return ("dve", o_dve[g])
        proj_pipe("wvvt", lambda cc, blk: sl1(cc)[:, blk * 512:(blk + 1) * 512],
                  post_o, NTB, 512)
        if "rwkv" in taps:
            for cc in range(CCN):
                dma(sl2(cc), tap_aps["rwkv"][cc], sem="st",
                    waits=[("dve", cur("dve"))])
        barrier()

        tr6 = {}
        f6 = {}
        for ts in range(NTS):
            lw = allw() + [("st4r%d" % r, cur("st4r%d" % r))
                           for r in range(4)]
            if ts >= 1:
                lw = [("pe", tr6[ts - 1])]
            for cc in range(CCN):
                dma(x1f_h[cc, :, ts * 128:(ts + 1) * 128],
                    TCm[:, cc * 128:(cc + 1) * 128], sem="ld6x", waits=lw)
                lw = ()
            l6 = cur("ld6x")
            la = dma(attn_h[ts * 128:(ts + 1) * 128, :], TA[:, :1024],
                     sem="ld6a", waits=[("out", cur("out"))])

            def e_tr6(g, t=ts):
                g.drain()
                mm = None
                for cc in range(CCN):
                    b, o = cc // 4, (cc % 4) * 128
                    mm = g.transpose(ps[b][:, o:o + 128],
                                     TCm[:, cc * 128:(cc + 1) * 128], eye[:])
                for cc in range(CCN):
                    b, o = 2 + cc // 4, (cc % 4) * 128
                    mm = g.transpose(ps[b][:, o:o + 128],
                                     sl2(cc)[:, t * 128:(t + 1) * 128], eye[:])
                return mm
            w6 = [("ld6x", l6)]
            if ts == 0:
                w6 += allw()
            else:
                w6.append(("dve", f6[ts - 1]))
            op("pe", e_tr6, waits=w6, incs=[("pe", 1)])
            tr6[ts] = bump("pe")

            def e_f(v):
                v.drain()
                v.tensor_tensor(TA[:, 0:512], TA[:, 0:512], ps[0][:, :512],
                                ALU.add)
                v.drain()
                v.tensor_tensor(TA[:, 512:1024], TA[:, 512:1024],
                                ps[1][:, :512], ALU.add)
                v.drain()
                v.tensor_tensor(TA[:, 0:512], TA[:, 0:512], ps[2][:, :512],
                                ALU.add)
                v.drain()
                return v.tensor_tensor(TA[:, 512:1024], TA[:, 512:1024],
                                       ps[3][:, :512], ALU.add)
            op("dve", e_f, waits=[("pe", tr6[ts]), ("ld6a", la)],
               incs=[("dve", 1)])
            f6[ts] = bump("dve")
            dma(TA[:, :1024], out[ts * 128:(ts + 1) * 128, :], sem="out",
                waits=[("dve", f6[ts])])
        op("sp", None, waits=[("out", cur("out"))] +
           [(s, cur(s)) for s in cnt if s.startswith("st")])

        if upto is not None:
            del ops[stage_marks[upto]:]
        sems = {}
        for (eng, waits, emit, incs) in ops:
            for (s, v) in waits:
                if s not in sems:
                    sems[s] = es.enter_context(nc.semaphore(s))
            for (s, n) in incs:
                if s not in sems:
                    sems[s] = es.enter_context(nc.semaphore(s))

        block = es.enter_context(nc.Block())

        def replay(role, eng):
            for (e, waits, emit, incs) in ops:
                if e != role:
                    continue
                for (s, v) in waits:
                    if v > 0:
                        eng.wait_ge(sems[s], v)
                instr = emit(eng) if emit is not None else None
                for (s, n) in incs:
                    assert instr is not None
                    instr.then_inc(sems[s], n)

        @block.tensor
        def _(pe):
            replay("pe", pe)

        @block.vector
        def _(dve):
            replay("dve", dve)

        @block.scalar
        def _(act):
            replay("act", act)

        @block.gpsimd
        def _(pool):
            replay("pool", pool)

        @block.sync
        def _(sp):
            replay("sp", sp)

    return nc, tap_aps



def _preslice(wt):
    return np.ascontiguousarray(
        wt.reshape(CCN, 128, CCN, 128).transpose(2, 0, 1, 3))


def _host_inputs(inputs, TC, core):
    b, j = core
    x = np.asarray(inputs["L0_attn_x"])
    B, L, _ = x.shape
    t0 = j * TC
    xs = np.zeros((TC + 2, C), np.float32)
    xs[1:TC + 1] = x[b, t0:t0 + TC]
    if t0 > 0:
        xs[0] = x[b, t0 - 1]
    if t0 + TC < L:
        xs[TC + 1] = x[b, t0 + TC]
    T = lambda w: _preslice(np.ascontiguousarray(np.asarray(w, np.float32).T))
    nj = L // TC
    selw = np.zeros(8, np.float32)
    selm = np.zeros(8, np.float32)
    for i in range(nj):
        gi = b * nj + i
        if i < j:
            selm[gi] = 1.0
            selw[gi] = float(TC * (j - 1 - i))
    kvsel = np.zeros(8, np.float32)
    for i in range(nj):
        kvsel[b * nj + i] = 1.0
    pvsel = np.zeros(8, np.float32)
    if j > 0:
        pvsel[b * nj + j - 1] = 1.0
    hmask = np.array([0.0 if j == 0 else 1.0,
                      0.0 if j == nj - 1 else 1.0], np.float32)
    rep = lambda v: np.ascontiguousarray(
        np.broadcast_to(np.asarray(v, np.float32)[None, :], (128, len(v))))
    return {
        "xfh": np.ascontiguousarray(xs.T.reshape(CCN, 128, TC + 2)),
        "w1t": T(inputs["L0_attn_w1_w"]), "v1t": T(inputs["L0_attn_v1_w"]),
        "w2t": T(inputs["L0_attn_w2_w"]),
        "wd": np.ascontiguousarray(np.asarray(inputs["L0_ffn_weight"],
                                              np.float32)),
        "wqt": T(inputs["L1_attn_Wq"]), "wkt": T(inputs["L1_attn_Wk"]),
        "wvt": T(inputs["L1_attn_Wv"]), "wrt": T(inputs["L1_ffn_W_r"]),
        "wkkt": T(inputs["L1_ffn_W_k"]), "wvvt": T(inputs["L1_ffn_W_v"]),
        "b_w1": np.asarray(inputs["L0_attn_w1_b"], np.float32),
        "b_v1": np.asarray(inputs["L0_attn_v1_b"], np.float32),
        "b_w2": np.asarray(inputs["L0_attn_w2_b"], np.float32),
        "b_f": np.asarray(inputs["L0_ffn_bias"], np.float32),
        "b_q": np.asarray(inputs["L1_attn_bq"], np.float32),
        "b_k": np.asarray(inputs["L1_attn_bk"], np.float32),
        "b_v": np.asarray(inputs["L1_attn_bv"], np.float32),
        "gamma": np.asarray(inputs["L1_attn_gamma"], np.float32),
        "beta": np.asarray(inputs["L1_attn_beta"], np.float32),
        "mu_r": np.asarray(inputs["L1_ffn_mu_r"], np.float32),
        "mu_k": np.asarray(inputs["L1_ffn_mu_k"], np.float32),
        "dssm": np.asarray(inputs["L0_attn_D_ssm"], np.float32),
        "gamma_row": np.asarray(inputs["L1_attn_gamma"], np.float32)[None, :],
        "beta_row": np.asarray(inputs["L1_attn_beta"], np.float32)[None, :],
        "cw": np.ascontiguousarray(
            np.asarray(inputs["L0_attn_conv_w"], np.float32)[:, 0, :]),
        "A_": np.asarray(inputs["L0_attn_A"], np.float32),
        "B_": np.asarray(inputs["L0_attn_B_ssm"], np.float32),
        "C_": np.asarray(inputs["L0_attn_C_ssm"], np.float32),
        "eye": np.eye(128, dtype=np.float32),
        "selw": rep(selw), "selm": rep(selm), "kvsel": rep(kvsel),
        "pvsel": rep(pvsel), "hmask": rep(hmask),
    }


def kernel(**inputs):
    x = np.asarray(inputs["L0_attn_x"])
    B, L, _ = x.shape
    ncore = 8
    nj = ncore // B
    TC = L // nj
    nc, _ = build(TC)
    in_maps = [_host_inputs(inputs, TC, (i // nj, i % nj)) for i in range(ncore)]
    res = run_bass_kernel_spmd(nc, in_maps, core_ids=list(range(ncore)))
    outp = np.empty((B, L, C), np.float32)
    for i in range(ncore):
        b, j = i // nj, i % nj
        outp[b, j * TC:(j + 1) * TC] = np.asarray(res.results[i]["out"])
    return outp
